# version 02
# baseline (speedup 1.0000x reference)
import contextlib
import numpy as np
import concourse.bass as bass
import concourse.mybir as mybir
from concourse.bass_utils import run_bass_kernel_spmd

F32 = mybir.dt.float32
BF16 = mybir.dt.bfloat16
F8 = mybir.dt.float8e4
U8 = mybir.dt.uint8
AF = mybir.ActivationFunctionType
ALU = mybir.AluOpType
AX = mybir.AxisListType

D = 2048
T = 1024
C = 16
TI = 8
H = 16
NB = 4
FI = 6144
OC2 = 24
NSQ = 3
NSLOT = 16 + NSQ
LN_EPS = 1e-5
ATT_EPS = 1e-6


def build_nc(upto="all", taps=()):
    nc = bass.Bass("TRN2", target_bir_lowering=False)

    x_in = nc.dram_tensor("x", [T, D], F32, kind="ExternalInput").ap()
    cb_in = nc.dram_tensor("cb", [1, D], F32, kind="ExternalInput").ap()
    wq_in = nc.dram_tensor("wq", [D, D], F32, kind="ExternalInput").ap()
    wo_in = nc.dram_tensor("wo", [D, D], F32, kind="ExternalInput").ap()
    bw_in = nc.dram_tensor("bw", [H, NB], F32, kind="ExternalInput").ap()
    mem_in = nc.dram_tensor("mem", [NB, H, 128, 128], F32, kind="ExternalInput").ap()
    mn_in = nc.dram_tensor("mn", [NB, H, 128], F32, kind="ExternalInput").ap()
    gw_in = nc.dram_tensor("gw", [D, FI], F32, kind="ExternalInput").ap()
    uw_in = nc.dram_tensor("uw", [D, FI], F32, kind="ExternalInput").ap()
    dw_in = nc.dram_tensor("dw", [FI, D], F32, kind="ExternalInput").ap()
    gs_in = nc.dram_tensor("gs", [1, FI], F32, kind="ExternalInput").ap()
    us_in = nc.dram_tensor("us", [1, FI], F32, kind="ExternalInput").ap()
    ds_in = nc.dram_tensor("ds", [1, D], F32, kind="ExternalInput").ap()
    wm_in = nc.dram_tensor("wm", [D, 6 * D], F32, kind="ExternalInput").ap()
    cst_in = nc.dram_tensor("cst", [128, 129], F32, kind="ExternalInput").ap()
    out_d = nc.dram_tensor("out", [T, D], F32, kind="ExternalOutput").ap()
    modscr = nc.dram_tensor("modscr", [96, 128], BF16, kind="Internal").ap()
    cwscr = nc.dram_tensor("cwscr", [H, NB * T], BF16, kind="Internal").ap()

    tap_d = {}
    for name, shape, dt in taps:
        tap_d[name] = nc.dram_tensor("tap_" + name, list(shape), dt, kind="ExternalOutput").ap()

    cnt = {}
    rec = {}

    with contextlib.ExitStack() as _st:
        ec = _st.enter_context
        semA = ec(nc.semaphore("semA"))
        semT = ec(nc.semaphore("semT"))
        semV = ec(nc.semaphore("semV"))
        semC = ec(nc.semaphore("semC"))
        semX = ec(nc.semaphore("semX"))
        semLN = ec(nc.semaphore("semLN"))
        semTR = [ec(nc.semaphore("semTR0")), ec(nc.semaphore("semTR1"))]
        semOT = [ec(nc.semaphore("semOT0")), ec(nc.semaphore("semOT1"))]
        semST = ec(nc.semaphore("semST"))
        semG0 = ec(nc.semaphore("semG0"))
        semWM = [ec(nc.semaphore("semWM0")), ec(nc.semaphore("semWM1"))]
        semWQ = [ec(nc.semaphore("semWQ0")), ec(nc.semaphore("semWQ1"))]
        semME = ec(nc.semaphore("semME"))
        semG8 = ec(nc.semaphore("semG8"))
        semU8 = ec(nc.semaphore("semU8"))
        semD8 = ec(nc.semaphore("semD8"))
        semTP = ec(nc.semaphore("semTP"))
        Xs = ec(nc.sbuf_tensor("X", [128, TI * D], F32))
        hT = ec(nc.sbuf_tensor("hT", [128, C * T], BF16))
        htok = ec(nc.sbuf_tensor("htok", [128, D], BF16))
        httmp = ec(nc.sbuf_tensor("httmp", [128, D], BF16))
        httmp2 = ec(nc.sbuf_tensor("httmp2", [128, D], BF16))
        gagb = ec(nc.sbuf_tensor("gagb", [128, 32], F32))
        siluc = ec(nc.sbuf_tensor("siluc", [128, 16], F32))
        silucb = ec(nc.sbuf_tensor("silucb", [128, 16], BF16))
        stats6 = ec(nc.sbuf_tensor("stats6", [128, 24], F32))
        stats2 = ec(nc.sbuf_tensor("stats2", [128, 2], F32))
        lnsm = ec(nc.sbuf_tensor("lnsm", [128, 2], F32))
        onesb = ec(nc.sbuf_tensor("ones", [128, 1], F32))
        identb = ec(nc.sbuf_tensor("ident", [128, 128], F32))
        gsT = ec(nc.sbuf_tensor("gsT", [128, 48], F32))
        usT = ec(nc.sbuf_tensor("usT", [128, 48], F32))
        dsT = ec(nc.sbuf_tensor("dsT", [128, 16], F32))
        gds = ec(nc.sbuf_tensor("gds", [128, 16], F32))
        mnT = ec(nc.sbuf_tensor("mnT", [128, H * NB], BF16))
        wdT = ec(nc.sbuf_tensor("wdT", [4, 16], F32))
        activ = ec(nc.sbuf_tensor("activ", [4, 1], F32))
        pA = ec(nc.psum_tensor("pA", [128, 1024], F32))
        pB = ec(nc.psum_tensor("pB", [128, 1024], F32))
        pC = ec(nc.psum_tensor("pC", [128, 1024], F32))
        pD = ec(nc.psum_tensor("pD", [128, 1024], F32))
        block = ec(nc.Block())

        Xv = Xs.ap().rearrange("p (ti d) -> p ti d", ti=TI)
        hTv = hT.ap().rearrange("p (j t) -> p j t", j=C)

        def _n(sem, k):
            key = id(sem)
            cnt[key] = cnt.get(key, 0) + k
            return cnt[key]

        def _cur(sem):
            return cnt.get(id(sem), 0)

        def VOP(v, mk):
            v.wait_ge(semV, _cur(semV))
            mk().then_inc(semV, 1)
            return (semV, _n(semV, 1))

        def COP(s, mk):
            s.wait_ge(semC, _cur(semC))
            mk().then_inc(semC, 1)
            return (semC, _n(semC, 1))

        def DMA(sem, mk):
            mk().then_inc(sem, 16)
            return (sem, _n(sem, 16))

        def SDMA(g, sem, mk):
            g.wait_ge(sem, _cur(sem))
            return DMA(sem, mk)

        def MMGROUP(t, items):
            last = None
            for (o, l, r, st, sp) in items:
                last = t.matmul(o, l, r, start=st, stop=sp)
            last.then_inc(semT, 1)
            return (semT, _n(semT, 1))

        def W(eng, key):
            sem, val = rec[key]
            eng.wait_ge(sem, val)

        def emit_tap(gatekeys, srcs):
            @block.gpsimd
            def _(g):
                for kk in gatekeys:
                    W(g, kk)
                for name, shape, dt in taps:
                    if name in srcs:
                        DMA(semTP, lambda name=name: g.dma_start(
                            out=tap_d[name], in_=srcs[name]))
                g.wait_ge(semTP, _cur(semTP))

        def emit_ln(tag, lnc, lns, prev_tag, xgate):
            for ti in range(TI):
                @block.vector
                def _(v, ti=ti):
                    if ti == 0:
                        W(v, "x")
                        W(v, "%stiles" % tag)
                        if xgate is not None:
                            W(v, xgate)
                    xt = Xs[:, D * ti:D * (ti + 1)]
                    for gr in range(4):
                        VOP(v, lambda gr=gr: v.bn_stats(
                            stats6[:, 6 * gr:6 * gr + 6], xt[:, 512 * gr:512 * (gr + 1)]))
                    VOP(v, lambda: v.bn_aggr(stats2[:], stats6[:]))
                    rec["%seps%d" % (tag, ti)] = VOP(v, lambda: v.tensor_scalar(
                        lnsm[:, 0:1], stats2[:, 1:2], LN_EPS, None, ALU.add))

                @block.scalar
                def _(s, ti=ti):
                    W(s, "%seps%d" % (tag, ti))
                    rec["%ssq%d" % (tag, ti)] = COP(s, lambda: s.activation(
                        lnsm[:, 1:2], lnsm[:, 0:1], AF.Sqrt))

                @block.vector
                def _(v, ti=ti):
                    xt = Xs[:, D * ti:D * (ti + 1)]
                    W(v, "%ssq%d" % (tag, ti))
                    VOP(v, lambda: v.reciprocal(lnsm[:, 1:2], lnsm[:, 1:2]))
                    VOP(v, lambda: v.tensor_scalar(
                        httmp[:], xt, stats2[:, 0:1], lnsm[:, 1:2], ALU.subtract, ALU.mult))
                    VOP(v, lambda: v.tensor_mul(httmp2[:], httmp[:], lnc[:]))
                    if ti >= 1:
                        W(v, "%str%d" % (tag, ti - 1))
                    elif prev_tag is not None:
                        W(v, "%str%d" % (prev_tag, TI - 1))
                    rec["%sv%d" % (tag, ti)] = VOP(v, lambda: v.tensor_add(
                        htok[:], httmp2[:], lns[:]))

                @block.sync
                def _(y, ti=ti):
                    W(y, "%sv%d" % (tag, ti))
                    if tag == "ln2" and ti == 0:
                        W(y, "tq%d" % (H - 1))
                    rec["%str%d" % (tag, ti)] = DMA(semLN, lambda: y.dma_start_transpose(
                        hTv[:, :, 128 * ti:128 * (ti + 1)],
                        htok.ap().rearrange("p (j q) -> p j q", j=C)))

        _st0 = contextlib.ExitStack()
        ec0 = _st0.enter_context
        wmb = ec0(nc.sbuf_tensor("wmb", [128, 6 * D], F32))
        silucf = ec0(nc.sbuf_tensor("silucf", [128, 16], F32))
        bwS = ec0(nc.sbuf_tensor("bwS", [H, NB], F32))
        bwE = ec0(nc.sbuf_tensor("bwE", [H, NB], F32))
        bwR = ec0(nc.sbuf_tensor("bwR", [H, 1], F32))
        asum = ec0(nc.sbuf_tensor("asum", [128, 4], F32))
        modacc = ec0(nc.sbuf_tensor("modacc", [128, 96], F32))
        gtmp = ec0(nc.sbuf_tensor("gtmp", [128, 32], F32))
        modR = ec0(nc.sbuf_tensor("modR", [96, 128], BF16))
        lnc1 = ec0(nc.sbuf_tensor("lnc1", [128, D], BF16))
        lns1 = ec0(nc.sbuf_tensor("lns1", [128, D], BF16))

        @block.sync
        def _(y):
            rec["x"] = DMA(semX, lambda: y.dma_start(
                out=Xv, in_=x_in.rearrange("(ti p) d -> p ti d", p=128)))

        @block.gpsimd
        def _(g):
            DMA(semG0, lambda: g.dma_start(out=identb[:], in_=cst_in[:, 0:128]))
            DMA(semG0, lambda: g.dma_start(out=onesb[:], in_=cst_in[:, 128:129],
                                           allow_slow_non_contiguous=True))
            DMA(semG0, lambda: g.dma_start(
                out=siluc[:], in_=cb_in.rearrange("o (k p) -> p (o k)", p=128),
                allow_slow_non_contiguous=True))
            DMA(semG0, lambda: g.dma_start(out=bwS[:], in_=bw_in))
            for n in range(NB):
                DMA(semG0, lambda n=n: g.dma_start(
                    out=mnT.ap().rearrange("d (h n) -> d h n", n=NB)[:, :, n],
                    in_=mn_in[n].transpose([1, 0]), allow_slow_non_contiguous=True))
            DMA(semG0, lambda: g.dma_start(
                out=gsT[:], in_=gs_in.rearrange("o (oc q) -> q (o oc)", q=128),
                allow_slow_non_contiguous=True))
            DMA(semG0, lambda: g.dma_start(
                out=usT[:], in_=us_in.rearrange("o (oc q) -> q (o oc)", q=128),
                allow_slow_non_contiguous=True))
            rec["g0"] = DMA(semG0, lambda: g.dma_start(
                out=dsT[:], in_=ds_in.rearrange("o (pc q) -> q (o pc)", q=128),
                allow_slow_non_contiguous=True))

        @block.scalar
        def _(s):
            W(s, "g0")
            rec["silucb"] = COP(s, lambda: s.activation(silucf[:], siluc[:], AF.Silu))

        for k in range(C):
            @block.gpsimd
            def _(g, k=k):
                if k >= 1:
                    W(g, "modk%d" % (k - 1))
                rec["wmld%d" % k] = DMA(semWM[0], lambda: g.dma_start(
                    out=wmb[:], in_=wm_in[128 * k:128 * (k + 1), :]))

            @block.tensor
            def _(t, k=k):
                W(t, "wmld%d" % k)
                if k == 0:
                    W(t, "silucb")
                if k >= 2:
                    W(t, "macc%d" % (k - 2))
                bank = pC if k % 2 == 0 else pD
                last = None
                for oc in range(96):
                    last = t.matmul(
                        bank[:, oc:oc + 1],
                        wmb[:, 128 * oc:128 * (oc + 1)],
                        silucf[:, k:k + 1], start=True, stop=True)
                last.then_inc(semT, 1)
                rec["modk%d" % k] = (semT, _n(semT, 1))

            @block.vector
            def _(v, k=k):
                W(v, "modk%d" % k)
                bank = pC if k % 2 == 0 else pD
                v.wait_ge(semV, _cur(semV))
                if k == 0:
                    rec["macc%d" % k] = VOP(v, lambda: v.tensor_copy(
                        out=modacc[:], in_=bank[:, 0:96]))
                else:
                    rec["macc%d" % k] = VOP(v, lambda: v.tensor_add(
                        modacc[:], modacc[:], bank[:, 0:96]))

        @block.vector
        def _(v):
            mview = modacc.ap().rearrange("p (j k) -> p j k", k=16)
            gview = modacc.ap().rearrange("p (j k) -> p j k", k=16)[:, 2:6:3, :]
            VOP(v, lambda: v.tensor_scalar(
                mview[:, 1:5:3, :], mview[:, 1:5:3, :], 1.0, None, ALU.add))
            gv2 = gagb.ap().rearrange("p (j k) -> p j k", k=16)
            VOP(v, lambda: v.tensor_scalar(gv2, gview, 1e30, 1.0, ALU.mult, ALU.add))
            VOP(v, lambda: v.tensor_scalar(gagb[:], gagb[:], -1.0, 1.0, ALU.max, ALU.min))
            VOP(v, lambda: v.tensor_scalar(
                gtmp.ap().rearrange("p (j k) -> p j k", k=16), gview, -1.0, None, ALU.mult))
            VOP(v, lambda: v.tensor_tensor(
                gtmp.ap().rearrange("p (j k) -> p j k", k=16), gview,
                gtmp.ap().rearrange("p (j k) -> p j k", k=16), ALU.max))
            VOP(v, lambda: v.tensor_scalar_max(gtmp[:], gtmp[:], 0.25))
            rec["gagb"] = VOP(v, lambda: v.tensor_mul(gagb[:], gagb[:], gtmp[:]))
            W(v, "g0")
            rec["gds"] = VOP(v, lambda: v.tensor_mul(gds[:], dsT[:], gagb[:, 16:32]))
            rec["asum"] = VOP(v, lambda: v.reduce_sum(
                asum[:], mnT.ap().rearrange("d (h n) -> d n h", n=NB), axis=AX.X))
            VOP(v, lambda: v.reduce_max(bwR[:], bwS[:], axis=AX.X))
            rec["bwsub"] = VOP(v, lambda: v.tensor_scalar(
                bwE[:], bwS[:], bwR[:], None, ALU.subtract))

        @block.scalar
        def _(s):
            W(s, "bwsub")
            rec["bwexp"] = COP(s, lambda: s.activation(bwE[:], bwE[:], AF.Exp))

        @block.vector
        def _(v):
            W(v, "bwexp")
            VOP(v, lambda: v.reduce_sum(bwR[:], bwE[:], axis=AX.X))
            VOP(v, lambda: v.reciprocal(bwR[:], bwR[:]))
            rec["bwsm"] = VOP(v, lambda: v.tensor_scalar(
                bwE[:], bwE[:], bwR[:], None, ALU.mult))

        @block.tensor
        def _(t):
            W(t, "gds")
            t.transpose(pA[0:96, 0:128], modacc[:], identb[:]).then_inc(semT, 1)
            rec["modtr"] = (semT, _n(semT, 1))
            W(t, "bwsm")
            t.transpose(pB[0:4, 0:16], bwE[0:16, 0:4], identb[0:16, 0:16]).then_inc(semT, 1)
            rec["wdtr"] = (semT, _n(semT, 1))
            W(t, "asum")
            t.matmul(pB[0:4, 16:17], asum[:], onesb[:], start=True, stop=True).then_inc(semT, 1)
            rec["actmm"] = (semT, _n(semT, 1))

        @block.vector
        def _(v):
            W(v, "actmm")
            rec["modRc"] = VOP(v, lambda: v.tensor_copy(out=modR[:], in_=pA[0:96, 0:128]))
            rec["wdTc"] = VOP(v, lambda: v.tensor_copy(out=wdT[:], in_=pB[0:4, 0:16]))
            VOP(v, lambda: v.tensor_scalar(
                activ[:], pB[0:4, 16:17], ATT_EPS, 1e30, ALU.subtract, ALU.mult))
            rec["activ"] = VOP(v, lambda: v.tensor_scalar(
                activ[:], activ[:], 1.0, 0.0, ALU.add, ALU.max))
            rec["activ"] = VOP(v, lambda: v.tensor_scalar_min(activ[:], activ[:], 1.0))

        @block.scalar
        def _(s):
            W(s, "modRc")
            DMA(semA, lambda: s.dma_start(out=modscr, in_=modR[:]))
            s.wait_ge(semA, _cur(semA))
            DMA(semA, lambda: s.dma_start(out=lns1[:], in_=modscr[0:16, :].rearrange(
                "r e -> (r e)").unsqueeze(0).to_broadcast([128, D])))
            rec["ln1tiles"] = DMA(semA, lambda: s.dma_start(
                out=lnc1[:], in_=modscr[16:32, :].rearrange(
                    "r e -> (r e)").unsqueeze(0).to_broadcast([128, D])))

        emit_ln("ln1", lnc1, lns1, None, None)

        if upto == "mod":
            emit_tap(["activ", "ln1tr%d" % (TI - 1)],
                     dict(modacc=modacc[:], gagb=gagb[:], wdT=wdT[:], activ=activ[:],
                          h1T=hT[:], X=Xs[:]))
            _st0.close()
            return nc

        _st0.close()

        if upto == "h1":
            emit_tap(["ln1tr%d" % (TI - 1)], dict(h1T=hT[:], X=Xs[:]))
            return nc

        _st1 = contextlib.ExitStack()
        ec1 = _st1.enter_context
        sqc = ec1(nc.sbuf_tensor("sqc", [128, NSLOT * T], BF16))
        wsraw = ec1(nc.sbuf_tensor("wsraw", [128, 2 * D], BF16))
        wstr = ec1(nc.sbuf_tensor("wstr", [128, 2 * D], BF16))
        mem4 = ec1(nc.sbuf_tensor("mem4", [128, 4 * 512], BF16))
        cwb = ec1(nc.sbuf_tensor("cwb", [128, NB * T], BF16))
        cwbf = ec1(nc.sbuf_tensor("cwbf", [4, T], BF16))
        nrme = ec1(nc.sbuf_tensor("nrme", [4, T], BF16))
        nrmr = ec1(nc.sbuf_tensor("nrmr", [4, T], BF16))
        mbuf = ec1(nc.sbuf_tensor("mbuf", [128, T], BF16))
        rbuf = ec1(nc.sbuf_tensor("rbuf", [128, T], BF16))
        ebuf = ec1(nc.sbuf_tensor("ebuf", [128, T], BF16))
        sqs4 = ec1(nc.sbuf_tensor("sqs4", [128, NB * T], BF16))
        outTe = ec1(nc.sbuf_tensor("outTe", [128, 2 * T], BF16))
        otok = ec1(nc.sbuf_tensor("otok", [128, 2 * T], BF16))

        def sqslot(h):
            return sqc[:, T * (16 + h % NSQ):T * (17 + h % NSQ)]

        def combslot(j):
            return sqc[:, T * j:T * (j + 1)]

        @block.gpsimd
        def _(g):
            W(g, "actmm")
            W(g, "activ")
            W(g, "bwexp")
            for h in range(2):
                rec["wqld%d" % h] = DMA(semWQ[h % 2], lambda h=h: g.dma_start(
                    out=wsraw[:, D * (h % 2):D * (h % 2 + 1)],
                    in_=wq_in[128 * h:128 * (h + 1), :]))

        for u in range(H + 4):
            @block.vector
            def _(v, u=u):
                if 4 <= u and u - 4 < H:
                    h = u - 4
                    W(v, "tY%d" % h)
                    rec["vcomb%d" % h] = VOP(v, lambda: v.tensor_copy(
                        out=combslot(h), in_=pD[:, :]))
                if 3 <= u and u - 3 < H:
                    h = u - 3
                    W(v, "acw%d" % h)
                    r = None
                    for n in range(NB):
                        r = VOP(v, lambda n=n: v.tensor_mul(
                            sqs4[:, T * n:T * (n + 1)], sqslot(h),
                            cwb[:, T * n:T * (n + 1)]))
                    rec["vsqs%d" % h] = r
                if 1 <= u and u - 1 < H:
                    h = u - 1
                    W(v, "tq%d" % h)
                    if h >= 1:
                        W(v, "cexp%d" % (h - 1))
                    qp = pA if h % 2 == 0 else pB
                    rec["vmin%d" % h] = VOP(v, lambda: v.tensor_scalar_min(
                        mbuf[:], qp[:, :], 0.0))

            @block.scalar
            def _(s, u=u):
                if 1 <= u and u - 1 < H:
                    h = u - 1
                    W(s, "tq%d" % h)
                    if h >= 1:
                        W(s, "vsq%d" % (h - 1))
                    qp = pA if h % 2 == 0 else pB
                    COP(s, lambda: s.activation(rbuf[:], qp[:, :], AF.Relu))
                    W(s, "vmin%d" % h)
                    rec["cexp%d" % h] = COP(s, lambda: s.activation(ebuf[:], mbuf[:], AF.Exp))

            @block.vector
            def _(v, u=u):
                if 1 <= u and u - 1 < H:
                    h = u - 1
                    W(v, "cexp%d" % h)
                    if h == 0:
                        W(v, "modk%d" % (C - 1))
                    rec["vsq%d" % h] = VOP(v, lambda: v.tensor_add(
                        sqslot(h), ebuf[:], rbuf[:]))
                if 2 <= u and u - 2 < H:
                    h = u - 2
                    W(v, "tn%d" % h)
                    rec["vnmax%d" % h] = VOP(v, lambda: v.tensor_scalar_max(
                        nrme[:], pC[0:4, :], ATT_EPS))
                    with nc.allow_low_precision(reason="bank-norm recip at bf16"):
                        VOP(v, lambda: v.reciprocal(nrmr[:], nrme[:]))
                    rec["vcw%d" % h] = VOP(v, lambda: v.tensor_scalar(
                        cwbf[:], nrmr[:], wdT[:, h:h + 1], activ[:], ALU.mult, ALU.mult))

            @block.scalar
            def _(s, u=u):
                if 2 <= u and u - 2 < H:
                    h = u - 2
                    W(s, "vcw%d" % h)
                    DMA(semA, lambda: s.dma_start(
                        out=cwscr[h:h + 1, :].rearrange("o (n t) -> (o n) t", n=NB),
                        in_=cwbf[:]))
                    s.wait_ge(semA, _cur(semA))
                    if h >= 1:
                        W(s, "vsqs%d" % (h - 1))
                    rec["acw%d" % h] = DMA(semA, lambda: s.dma_start(
                        out=cwb[:], in_=cwscr[h:h + 1, :].to_broadcast([128, NB * T])))

            @block.sync
            def _(y, u=u):
                if u < H:
                    h = u
                    W(y, "wqld%d" % h)
                    if h >= 2:
                        W(y, "tq%d" % (h - 2))
                    rec["wqtr%d" % h] = DMA(semTR[h % 2], lambda: y.dma_start_transpose(
                        wstr[:, D * (h % 2):D * (h % 2 + 1)].rearrange("p (k q) -> p k q", k=C),
                        wsraw[:, D * (h % 2):D * (h % 2 + 1)].rearrange("p (k q) -> p k q", k=C)))

            @block.tensor
            def _(t, u=u):
                if u < H:
                    h = u
                    W(t, "wqtr%d" % h)
                    if h == 0:
                        W(t, "ln1tr%d" % (TI - 1))
                        W(t, "activ")
                    if h >= 2:
                        W(t, "vmin%d" % (h - 2))
                        W(t, "cexp%d" % (h - 2))
                    qp = pA if h % 2 == 0 else pB
                    items = []
                    for k in range(C):
                        for hf in range(2):
                            items.append((
                                qp[:, 512 * hf:512 * (hf + 1)],
                                wstr[:, D * (h % 2) + 128 * k:D * (h % 2) + 128 * (k + 1)],
                                hTv[:, k, 512 * hf:512 * (hf + 1)],
                                k == 0, k == C - 1))
                    rec["tq%d" % h] = MMGROUP(t, items)
                if 1 <= u <= H:
                    h = u - 1
                    W(t, "vsq%d" % h)
                    if h >= 1:
                        W(t, "vnmax%d" % (h - 1))
                    items = [(pC[0:4, 512 * hf:512 * (hf + 1)],
                              mnT[:, 4 * h:4 * (h + 1)],
                              sqslot(h)[:, 512 * hf:512 * (hf + 1)],
                              True, True) for hf in range(2)]
                    rec["tn%d" % h] = MMGROUP(t, items)
                if 3 <= u and u - 3 < H:
                    h = u - 3
                    W(t, "vsqs%d" % h)
                    t.wait_ge(semME, 16 * (h + 1))
                    if h >= 1:
                        W(t, "vcomb%d" % (h - 1))
                    items = []
                    for n in range(NB):
                        for hf in range(2):
                            items.append((
                                pD[:, 512 * hf:512 * (hf + 1)],
                                mem4[:, 512 * (h % 4) + 128 * n:512 * (h % 4) + 128 * (n + 1)],
                                sqs4[:, T * n + 512 * hf:T * n + 512 * (hf + 1)],
                                n == 0, n == NB - 1))
                    rec["tY%d" % h] = MMGROUP(t, items)

            @block.gpsimd
            def _(g, u=u):
                if u + 2 < H:
                    h = u + 2
                    W(g, "wqtr%d" % (h - 2))
                    rec["wqld%d" % h] = DMA(semWQ[h % 2], lambda: g.dma_start(
                        out=wsraw[:, D * (h % 2):D * (h % 2 + 1)],
                        in_=wq_in[128 * h:128 * (h + 1), :]))
                if u < H:
                    h = u
                    if h >= 4:
                        W(g, "tY%d" % (h - 4))
                    rec["memld%d" % h] = SDMA(g, semME, lambda: g.dma_start(
                        out=mem4[:, 512 * (h % 4):512 * (h % 4 + 1)].rearrange(
                            "d (n e) -> d n e", n=NB),
                        in_=mem_in[:, h].transpose([1, 0, 2])))

        if upto == "attn":
            emit_tap(["vcomb%d" % (H - 1)], dict(comb=sqc[:, 0:16 * T], X=Xs[:]))
            _st1.close()
            return nc

        for u in range(H + 2):
            @block.vector
            def _(v, u=u):
                if 1 <= u <= H:
                    p = u - 1
                    W(v, "two%d" % p)
                    wp = pA if p % 2 == 0 else pB
                    rec["vwoe%d" % p] = VOP(v, lambda: v.tensor_scalar(
                        outTe[:, T * (p % 2):T * (p % 2 + 1)], wp[:, :],
                        gagb[:, p:p + 1], None, ALU.mult))
                if u >= 2:
                    p = u - 2
                    W(v, "wotr2%d" % p)
                    rec["vxadd%d" % p] = VOP(v, lambda: v.tensor_add(
                        Xv[:, :, 128 * p:128 * (p + 1)],
                        Xv[:, :, 128 * p:128 * (p + 1)],
                        otok[:, T * (p % 2):T * (p % 2 + 1)].rearrange(
                            "p (ti q) -> p ti q", ti=TI)))

            @block.gpsimd
            def _(g, u=u):
                if u < H:
                    p = u
                    if p < 2:
                        W(g, "wqtr%d" % (H - 2 + p))
                    else:
                        W(g, "wotr%d" % (p - 2))
                    rec["wold%d" % p] = DMA(semWQ[p % 2], lambda: g.dma_start(
                        out=wsraw[:, D * (p % 2):D * (p % 2 + 1)],
                        in_=wo_in[128 * p:128 * (p + 1), :]))

            @block.sync
            def _(y, u=u):
                if 1 <= u <= H:
                    p = u - 1
                    W(y, "vwoe%d" % p)
                    if p >= 2:
                        W(y, "vxadd%d" % (p - 2))
                    rec["wotr2%d" % p] = DMA(semOT[p % 2], lambda: y.dma_start_transpose(
                        otok[:, T * (p % 2):T * (p % 2 + 1)].rearrange("p (ti q) -> p ti q", ti=TI),
                        outTe[:, T * (p % 2):T * (p % 2 + 1)].rearrange("p (ti q) -> p ti q", ti=TI)))
                if u < H:
                    p = u
                    W(y, "wold%d" % p)
                    if p >= 2:
                        W(y, "two%d" % (p - 2))
                    rec["wotr%d" % p] = DMA(semTR[p % 2], lambda: y.dma_start_transpose(
                        wstr[:, D * (p % 2):D * (p % 2 + 1)].rearrange("p (k q) -> p k q", k=C),
                        wsraw[:, D * (p % 2):D * (p % 2 + 1)].rearrange("p (k q) -> p k q", k=C)))

            @block.tensor
            def _(t, u=u):
                if u < H:
                    p = u
                    W(t, "wotr%d" % p)
                    if p == 0:
                        W(t, "vcomb%d" % (H - 1))
                        W(t, "vmin%d" % (H - 1))
                        W(t, "cexp%d" % (H - 1))
                    if p >= 2:
                        W(t, "vwoe%d" % (p - 2))
                    wp = pA if p % 2 == 0 else pB
                    items = []
                    for j in range(C):
                        for hf in range(2):
                            items.append((
                                wp[:, 512 * hf:512 * (hf + 1)],
                                wstr[:, D * (p % 2) + 128 * j:D * (p % 2) + 128 * (j + 1)],
                                combslot(j)[:, 512 * hf:512 * (hf + 1)],
                                j == 0, j == C - 1))
                    rec["two%d" % p] = MMGROUP(t, items)

        _st1.close()

        if upto == "x1":
            emit_tap(["vxadd%d" % (H - 1)], dict(X=Xs[:]))
            return nc

        _st2 = contextlib.ExitStack()
        ec2 = _st2.enter_context
        lnc2 = ec2(nc.sbuf_tensor("lnc2", [128, D], BF16))
        lns2 = ec2(nc.sbuf_tensor("lns2", [128, D], BF16))
        fuse = ec2(nc.sbuf_tensor("fuse", [128, OC2 * T], BF16))
        g8 = ec2(nc.sbuf_tensor("g8", [128, 2 * D], U8))
        u8 = ec2(nc.sbuf_tensor("u8", [128, 2 * D], U8))
        d8 = ec2(nc.sbuf_tensor("d8", [128, 2 * OC2 * 128], U8))
        sgb = ec2(nc.sbuf_tensor("sg", [128, 2 * T], BF16))
        outTe2 = ec2(nc.sbuf_tensor("outTe2", [128, 2 * T], BF16))
        otok2 = ec2(nc.sbuf_tensor("otok2", [128, 2 * T], BF16))

        @block.scalar
        def _(s):
            W(s, "vxadd%d" % (H - 1))
            DMA(semA, lambda: s.dma_start(out=lns2[:], in_=modscr[48:64, :].rearrange(
                "r e -> (r e)").unsqueeze(0).to_broadcast([128, D])))
            rec["ln2tiles"] = DMA(semA, lambda: s.dma_start(
                out=lnc2[:], in_=modscr[64:80, :].rearrange(
                    "r e -> (r e)").unsqueeze(0).to_broadcast([128, D])))

        emit_ln("ln2", lnc2, lns2, "ln1", "vxadd%d" % (H - 1))

        if upto == "h2":
            emit_tap(["ln2tr%d" % (TI - 1)], dict(h2T=hT[:], X=Xs[:]))
            _st2.close()
            return nc

        for oh in range(2):
            for u in range(OC2 + 1):
                @block.scalar
                def _(s, u=u, oh=oh):
                    if 1 <= u:
                        oc = u - 1
                        ocg = OC2 * oh + oc
                        W(s, "tg_%d_%d" % (oh, oc))
                        rec["csil_%d_%d" % (oh, oc)] = COP(s, lambda: s.activation(
                            sgb[:, T * (oc % 2):T * (oc % 2 + 1)], pA[:, :], AF.Silu,
                            scale=gsT[:, ocg:ocg + 1]))

                @block.vector
                def _(v, u=u, oh=oh):
                    if 1 <= u:
                        oc = u - 1
                        ocg = OC2 * oh + oc
                        W(v, "tu_%d_%d" % (oh, oc))
                        W(v, "csil_%d_%d" % (oh, oc))
                        if oh == 1 and oc == 0:
                            W(v, "td_0_%d" % (H - 1))
                        rec["vfu_%d_%d" % (oh, oc)] = VOP(v, lambda: v.scalar_tensor_tensor(
                            fuse[:, T * oc:T * (oc + 1)], pB[:, :], usT[:, ocg:ocg + 1],
                            sgb[:, T * (oc % 2):T * (oc % 2 + 1)], ALU.mult, ALU.mult))

                @block.gpsimd
                def _(g, u=u, oh=oh):
                    if u < OC2:
                        oc = u
                        ocg = OC2 * oh + oc
                        if ocg == 0:
                            W(g, "two%d" % (H - 1))
                            W(g, "vxadd%d" % (H - 1))
                        if ocg >= 2:
                            po, pc_ = (ocg - 2) // OC2, (ocg - 2) % OC2
                            W(g, "tg_%d_%d" % (po, pc_))
                        rec["g8_%d_%d" % (oh, oc)] = SDMA(g, semG8, lambda: g.dma_start(
                            out=g8[:, D * (oc % 2):D * (oc % 2 + 1)].bitcast(F8).rearrange(
                                "q (k e) -> q k e", k=C),
                            in_=gw_in[:, 128 * ocg:128 * (ocg + 1)].rearrange(
                                "(k q) e -> q k e", q=128)))
                        if ocg >= 2:
                            po, pc_ = (ocg - 2) // OC2, (ocg - 2) % OC2
                            W(g, "tu_%d_%d" % (po, pc_))
                        rec["u8_%d_%d" % (oh, oc)] = SDMA(g, semU8, lambda: g.dma_start(
                            out=u8[:, D * (oc % 2):D * (oc % 2 + 1)].bitcast(F8).rearrange(
                                "q (k e) -> q k e", k=C),
                            in_=uw_in[:, 128 * ocg:128 * (ocg + 1)].rearrange(
                                "(k q) e -> q k e", q=128)))

                @block.tensor
                def _(t, u=u, oh=oh):
                    if u < OC2:
                        oc = u
                        W(t, "g8_%d_%d" % (oh, oc))
                        if oh + oc == 0:
                            W(t, "ln2tr%d" % (TI - 1))
                            W(t, "vwoe%d" % (H - 1))
                        if oc >= 1 or oh == 1:
                            po, pc_ = (oh, oc - 1) if oc >= 1 else (0, OC2 - 1)
                            W(t, "csil_%d_%d" % (po, pc_))
                        items = []
                        for k in range(C):
                            for hf in range(2):
                                items.append((
                                    pA[:, 512 * hf:512 * (hf + 1)],
                                    g8[:, D * (oc % 2) + 128 * k:D * (oc % 2) + 128 * (k + 1)].bitcast(F8),
                                    hTv[:, k, 512 * hf:512 * (hf + 1)],
                                    k == 0, k == C - 1))
                        rec["tg_%d_%d" % (oh, oc)] = MMGROUP(t, items)
                        W(t, "u8_%d_%d" % (oh, oc))
                        if oc >= 1 or oh == 1:
                            po, pc_ = (oh, oc - 1) if oc >= 1 else (0, OC2 - 1)
                            W(t, "vfu_%d_%d" % (po, pc_))
                        items = []
                        for k in range(C):
                            for hf in range(2):
                                items.append((
                                    pB[:, 512 * hf:512 * (hf + 1)],
                                    u8[:, D * (oc % 2) + 128 * k:D * (oc % 2) + 128 * (k + 1)].bitcast(F8),
                                    hTv[:, k, 512 * hf:512 * (hf + 1)],
                                    k == 0, k == C - 1))
                        rec["tu_%d_%d" % (oh, oc)] = MMGROUP(t, items)

            for u in range(H + 2):
                @block.vector
                def _(v, u=u, oh=oh):
                    if 1 <= u <= H:
                        p = u - 1
                        W(v, "td_%d_%d" % (oh, p))
                        dp = pC if p % 2 == 0 else pD
                        rec["vde_%d_%d" % (oh, p)] = VOP(v, lambda: v.tensor_scalar(
                            outTe2[:, T * (p % 2):T * (p % 2 + 1)], dp[:, :],
                            gds[:, p:p + 1], None, ALU.mult))
                    if u >= 2:
                        p = u - 2
                        W(v, "dtr_%d_%d" % (oh, p))
                        rec["vda_%d_%d" % (oh, p)] = VOP(v, lambda: v.tensor_add(
                            Xv[:, :, 128 * p:128 * (p + 1)],
                            Xv[:, :, 128 * p:128 * (p + 1)],
                            otok2[:, T * (p % 2):T * (p % 2 + 1)].rearrange(
                                "p (ti q) -> p ti q", ti=TI)))

                @block.gpsimd
                def _(g, u=u, oh=oh):
                    if u < H:
                        p = u
                        if oh == 1 or p >= 2:
                            po, pp = (oh, p - 2) if p >= 2 else (0, H - 2 + p)
                            W(g, "td_%d_%d" % (po, pp))
                        rec["d8_%d_%d" % (oh, p)] = SDMA(g, semD8, lambda: g.dma_start(
                            out=d8[:, OC2 * 128 * (p % 2):OC2 * 128 * (p % 2 + 1)].bitcast(F8).rearrange(
                                "q (oc e) -> q oc e", oc=OC2),
                            in_=dw_in[3072 * oh:3072 * (oh + 1),
                                      128 * p:128 * (p + 1)].rearrange(
                                "(oc q) e -> q oc e", q=128)))

                @block.sync
                def _(y, u=u, oh=oh):
                    if 1 <= u <= H:
                        p = u - 1
                        W(y, "vde_%d_%d" % (oh, p))
                        if p >= 2:
                            W(y, "vda_%d_%d" % (oh, p - 2))
                        elif oh == 1:
                            W(y, "vda_0_%d" % (H - 2 + p))
                        rec["dtr_%d_%d" % (oh, p)] = DMA(semOT[p % 2], lambda: y.dma_start_transpose(
                            otok2[:, T * (p % 2):T * (p % 2 + 1)].rearrange("p (ti q) -> p ti q", ti=TI),
                            outTe2[:, T * (p % 2):T * (p % 2 + 1)].rearrange("p (ti q) -> p ti q", ti=TI)))
                    if u >= 2 and oh == 1:
                        p = u - 2
                        W(y, "vda_1_%d" % p)
                        rec["st%d" % p] = DMA(semST, lambda: y.dma_start(
                            out=out_d.rearrange("(ti q) d -> q ti d", q=128)[:, :, 128 * p:128 * (p + 1)],
                            in_=Xv[:, :, 128 * p:128 * (p + 1)]))

                @block.tensor
                def _(t, u=u, oh=oh):
                    if u < H:
                        p = u
                        W(t, "d8_%d_%d" % (oh, p))
                        if p == 0:
                            W(t, "vfu_%d_%d" % (oh, OC2 - 1))
                            if oh == 0:
                                W(t, "vnmax%d" % (H - 1))
                                W(t, "vcomb%d" % (H - 1))
                        if p >= 2:
                            W(t, "vde_%d_%d" % (oh, p - 2))
                        elif oh == 1:
                            W(t, "vde_0_%d" % (H - 2 + p))
                        dp = pC if p % 2 == 0 else pD
                        items = []
                        for oc in range(OC2):
                            for hf in range(2):
                                items.append((
                                    dp[:, 512 * hf:512 * (hf + 1)],
                                    d8[:, OC2 * 128 * (p % 2) + 128 * oc:
                                       OC2 * 128 * (p % 2) + 128 * (oc + 1)].bitcast(F8),
                                    fuse[:, T * oc + 512 * hf:T * oc + 512 * (hf + 1)],
                                    oc == 0, oc == OC2 - 1))
                        rec["td_%d_%d" % (oh, p)] = MMGROUP(t, items)

        _st2.close()

        @block.sync
        def _(y):
            y.wait_ge(semST, 16 * H)

    return nc


def _consts():
    cst = np.zeros((128, 129), np.float32)
    cst[:, 0:128] = np.eye(128, dtype=np.float32)
    cst[:, 128] = 1.0
    return cst


def kernel(L0_attn_hidden_states, L0_attn_Wq, L0_attn_Wk, L0_attn_Wv, L0_attn_Wo,
           L0_attn_bank_weights, L0_attn_memories, L0_attn_memory_norms,
           L0_ffn_gate_w, L0_ffn_up_w, L0_ffn_down_w, L0_ffn_gate_scale,
           L0_ffn_up_scale, L0_ffn_down_scale, L0_c, L0_wmod):
    nc = build_nc()
    asnp = lambda a: np.ascontiguousarray(np.asarray(a, dtype=np.float32))
    x = asnp(L0_attn_hidden_states)
    shared = {
        "wq": asnp(L0_attn_Wq), "wo": asnp(L0_attn_Wo),
        "bw": asnp(L0_attn_bank_weights), "mem": asnp(L0_attn_memories),
        "mn": asnp(L0_attn_memory_norms), "gw": asnp(L0_ffn_gate_w),
        "uw": asnp(L0_ffn_up_w), "dw": asnp(L0_ffn_down_w),
        "gs": asnp(L0_ffn_gate_scale).reshape(1, FI),
        "us": asnp(L0_ffn_up_scale).reshape(1, FI),
        "ds": asnp(L0_ffn_down_scale).reshape(1, D),
        "wm": asnp(L0_wmod), "cst": _consts(),
    }
    cfull = asnp(L0_c)
    in_maps = []
    for c in range(8):
        b, th = c // 2, c % 2
        m = dict(shared)
        m["x"] = np.ascontiguousarray(x[b, T * th:T * (th + 1), :])
        m["cb"] = np.ascontiguousarray(cfull[b:b + 1, :])
        in_maps.append(m)
    res = run_bass_kernel_spmd(nc, in_maps, core_ids=list(range(8)))
    out = np.empty((4, 2048, D), np.float32)
    for c in range(8):
        b, th = c // 2, c % 2
        out[b, T * th:T * (th + 1), :] = res.results[c]["out"]
    return out


# version 06
# speedup vs baseline: 1.2787x; 1.2787x over previous
import contextlib
import numpy as np
import concourse.bass as bass
import concourse.mybir as mybir
from concourse.bass_utils import run_bass_kernel_spmd

F32 = mybir.dt.float32
BF16 = mybir.dt.bfloat16
F8 = mybir.dt.float8e4
U8 = mybir.dt.uint8
AF = mybir.ActivationFunctionType
ALU = mybir.AluOpType
AX = mybir.AxisListType

D = 2048
T = 1024
C = 16
TI = 8
H = 16
NB = 4
FI = 6144
OC2 = 24
NSQ = 3
NSLOT = 16 + NSQ
LN_EPS = 1e-5
ATT_EPS = 1e-6


def build_nc(upto="all", taps=()):
    nc = bass.Bass("TRN2", target_bir_lowering=False)

    x_in = nc.dram_tensor("x", [T, D], F32, kind="ExternalInput").ap()
    cb_in = nc.dram_tensor("cb", [1, D], F32, kind="ExternalInput").ap()
    wq_in = nc.dram_tensor("wq", [D, D], F32, kind="ExternalInput").ap()
    wo_in = nc.dram_tensor("wo", [D, D], F32, kind="ExternalInput").ap()
    bw_in = nc.dram_tensor("bw", [H, NB], F32, kind="ExternalInput").ap()
    mem_in = nc.dram_tensor("mem", [NB, H, 128, 128], F32, kind="ExternalInput").ap()
    mn_in = nc.dram_tensor("mn", [NB, H, 128], F32, kind="ExternalInput").ap()
    gw_in = nc.dram_tensor("gw", [D, FI], F32, kind="ExternalInput").ap()
    uw_in = nc.dram_tensor("uw", [D, FI], F32, kind="ExternalInput").ap()
    dw_in = nc.dram_tensor("dw", [FI, D], F32, kind="ExternalInput").ap()
    gs_in = nc.dram_tensor("gs", [1, FI], F32, kind="ExternalInput").ap()
    us_in = nc.dram_tensor("us", [1, FI], F32, kind="ExternalInput").ap()
    ds_in = nc.dram_tensor("ds", [1, D], F32, kind="ExternalInput").ap()
    wm_in = nc.dram_tensor("wm", [D, 6 * D], F32, kind="ExternalInput").ap()
    cst_in = nc.dram_tensor("cst", [128, 129], F32, kind="ExternalInput").ap()
    out_d = nc.dram_tensor("out", [T, D], F32, kind="ExternalOutput").ap()
    modscr = nc.dram_tensor("modscr", [96, 128], BF16, kind="Internal").ap()
    cwscr = nc.dram_tensor("cwscr", [H, NB * T], BF16, kind="Internal").ap()

    tap_d = {}
    for name, shape, dt in taps:
        tap_d[name] = nc.dram_tensor("tap_" + name, list(shape), dt, kind="ExternalOutput").ap()

    cnt = {}
    rec = {}

    with contextlib.ExitStack() as _st:
        ec = _st.enter_context
        semA = ec(nc.semaphore("semA"))
        semT = ec(nc.semaphore("semT"))
        semV = ec(nc.semaphore("semV"))
        semC = ec(nc.semaphore("semC"))
        semX = ec(nc.semaphore("semX"))
        semLN = ec(nc.semaphore("semLN"))
        semTR = [ec(nc.semaphore("semTR0")), ec(nc.semaphore("semTR1"))]
        semOT = [ec(nc.semaphore("semOT0")), ec(nc.semaphore("semOT1"))]
        semST = ec(nc.semaphore("semST"))
        semG0 = ec(nc.semaphore("semG0"))
        semWM = [ec(nc.semaphore("semWM0")), ec(nc.semaphore("semWM1"))]
        semWQ = [ec(nc.semaphore("semWQ0")), ec(nc.semaphore("semWQ1"))]
        semME = ec(nc.semaphore("semME"))
        semG8 = ec(nc.semaphore("semG8"))
        semU8 = ec(nc.semaphore("semU8"))
        semD8 = ec(nc.semaphore("semD8"))
        semTP = ec(nc.semaphore("semTP"))
        Xs = ec(nc.sbuf_tensor("X", [128, TI * D], F32))
        hT = ec(nc.sbuf_tensor("hT", [128, C * T], BF16))
        htok = ec(nc.sbuf_tensor("htok", [128, D], BF16))
        httmp = ec(nc.sbuf_tensor("httmp", [128, D], BF16))
        httmp2 = ec(nc.sbuf_tensor("httmp2", [128, D], BF16))
        gagb = ec(nc.sbuf_tensor("gagb", [128, 32], F32))
        siluc = ec(nc.sbuf_tensor("siluc", [128, 16], F32))
        silucb = ec(nc.sbuf_tensor("silucb", [128, 16], BF16))
        stats6 = ec(nc.sbuf_tensor("stats6", [128, 24], F32))
        stats2 = ec(nc.sbuf_tensor("stats2", [128, 2], F32))
        lnsm = ec(nc.sbuf_tensor("lnsm", [128, 2], F32))
        onesb = ec(nc.sbuf_tensor("ones", [128, 1], F32))
        identb = ec(nc.sbuf_tensor("ident", [128, 128], F32))
        gsT = ec(nc.sbuf_tensor("gsT", [128, 48], F32))
        usT = ec(nc.sbuf_tensor("usT", [128, 48], F32))
        dsT = ec(nc.sbuf_tensor("dsT", [128, 16], F32))
        gds = ec(nc.sbuf_tensor("gds", [128, 16], F32))
        mnT = ec(nc.sbuf_tensor("mnT", [128, H * NB], BF16))
        wdT = ec(nc.sbuf_tensor("wdT", [4, 16], F32))
        activ = ec(nc.sbuf_tensor("activ", [4, 1], F32))
        pA = ec(nc.psum_tensor("pA", [128, 1024], F32))
        pB = ec(nc.psum_tensor("pB", [128, 1024], F32))
        pC = ec(nc.psum_tensor("pC", [128, 1024], F32))
        pD = ec(nc.psum_tensor("pD", [128, 1024], F32))
        block = ec(nc.Block())

        Xv = Xs.ap().rearrange("p (ti d) -> p ti d", ti=TI)
        hTv = hT.ap().rearrange("p (j t) -> p j t", j=C)

        def _n(sem, k):
            key = id(sem)
            cnt[key] = cnt.get(key, 0) + k
            return cnt[key]

        def _cur(sem):
            return cnt.get(id(sem), 0)

        def VOP(v, mk):
            v.wait_ge(semV, _cur(semV))
            mk().then_inc(semV, 1)
            return (semV, _n(semV, 1))

        def COP(s, mk):
            s.wait_ge(semC, _cur(semC))
            mk().then_inc(semC, 1)
            return (semC, _n(semC, 1))

        def DMA(sem, mk):
            mk().then_inc(sem, 16)
            return (sem, _n(sem, 16))

        def SDMA(g, sem, mk):
            g.wait_ge(sem, _cur(sem))
            return DMA(sem, mk)

        def MMGROUP(t, items):
            last = None
            for (o, l, r, st, sp) in items:
                last = t.matmul(o, l, r, start=st, stop=sp)
            last.then_inc(semT, 1)
            return (semT, _n(semT, 1))

        def W(eng, key):
            sem, val = rec[key]
            eng.wait_ge(sem, val)

        def emit_tap(gatekeys, srcs):
            @block.gpsimd
            def _(g):
                for kk in gatekeys:
                    W(g, kk)
                for name, shape, dt in taps:
                    if name in srcs:
                        DMA(semTP, lambda name=name: g.dma_start(
                            out=tap_d[name], in_=srcs[name]))
                g.wait_ge(semTP, _cur(semTP))

        def emit_ln(tag, lnc, lns, prev_tag, xgate):
            for ti in range(TI):
                @block.vector
                def _(v, ti=ti):
                    if ti == 0:
                        W(v, "x")
                        W(v, "%stiles" % tag)
                        if xgate is not None:
                            W(v, xgate)
                    xt = Xs[:, D * ti:D * (ti + 1)]
                    for gr in range(4):
                        VOP(v, lambda gr=gr: v.bn_stats(
                            stats6[:, 6 * gr:6 * gr + 6], xt[:, 512 * gr:512 * (gr + 1)]))
                    VOP(v, lambda: v.bn_aggr(stats2[:], stats6[:]))
                    rec["%seps%d" % (tag, ti)] = VOP(v, lambda: v.tensor_scalar(
                        lnsm[:, 0:1], stats2[:, 1:2], LN_EPS, None, ALU.add))

                @block.scalar
                def _(s, ti=ti):
                    W(s, "%seps%d" % (tag, ti))
                    rec["%ssq%d" % (tag, ti)] = COP(s, lambda: s.activation(
                        lnsm[:, 1:2], lnsm[:, 0:1], AF.Sqrt))

                @block.vector
                def _(v, ti=ti):
                    xt = Xs[:, D * ti:D * (ti + 1)]
                    W(v, "%ssq%d" % (tag, ti))
                    VOP(v, lambda: v.reciprocal(lnsm[:, 1:2], lnsm[:, 1:2]))
                    VOP(v, lambda: v.tensor_scalar(
                        httmp[:], xt, stats2[:, 0:1], lnsm[:, 1:2], ALU.subtract, ALU.mult))
                    VOP(v, lambda: v.tensor_mul(httmp2[:], httmp[:], lnc[:]))
                    if ti >= 1:
                        W(v, "%str%d" % (tag, ti - 1))
                    elif prev_tag is not None:
                        W(v, "%str%d" % (prev_tag, TI - 1))
                    rec["%sv%d" % (tag, ti)] = VOP(v, lambda: v.tensor_add(
                        htok[:], httmp2[:], lns[:]))

                @block.sync
                def _(y, ti=ti):
                    W(y, "%sv%d" % (tag, ti))
                    if tag == "ln2" and ti == 0:
                        W(y, "tq%d" % (H - 1))
                    rec["%str%d" % (tag, ti)] = DMA(semLN, lambda: y.dma_start_transpose(
                        hTv[:, :, 128 * ti:128 * (ti + 1)],
                        htok.ap().rearrange("p (j q) -> p j q", j=C)))

        _st0 = contextlib.ExitStack()
        ec0 = _st0.enter_context
        wmb = ec0(nc.sbuf_tensor("wmb", [128, 6 * D], BF16))
        silucf = ec0(nc.sbuf_tensor("silucf", [128, 16], BF16))
        bwS = ec0(nc.sbuf_tensor("bwS", [H, NB], F32))
        bwE = ec0(nc.sbuf_tensor("bwE", [H, NB], F32))
        bwR = ec0(nc.sbuf_tensor("bwR", [H, 1], F32))
        asum = ec0(nc.sbuf_tensor("asum", [128, 4], F32))
        modacc = ec0(nc.sbuf_tensor("modacc", [128, 96], F32))
        gtmp = ec0(nc.sbuf_tensor("gtmp", [128, 32], F32))
        modR = ec0(nc.sbuf_tensor("modR", [96, 128], BF16))
        lnc1 = ec0(nc.sbuf_tensor("lnc1", [128, D], BF16))
        lns1 = ec0(nc.sbuf_tensor("lns1", [128, D], BF16))

        @block.sync
        def _(y):
            rec["x"] = DMA(semX, lambda: y.dma_start(
                out=Xv, in_=x_in.rearrange("(ti p) d -> p ti d", p=128)))

        @block.gpsimd
        def _(g):
            DMA(semG0, lambda: g.dma_start(out=identb[:], in_=cst_in[:, 0:128]))
            DMA(semG0, lambda: g.dma_start(out=onesb[:], in_=cst_in[:, 128:129],
                                           allow_slow_non_contiguous=True))
            DMA(semG0, lambda: g.dma_start(
                out=siluc[:], in_=cb_in.rearrange("o (k p) -> p (o k)", p=128),
                allow_slow_non_contiguous=True))
            DMA(semG0, lambda: g.dma_start(out=bwS[:], in_=bw_in))
            for n in range(NB):
                DMA(semG0, lambda n=n: g.dma_start(
                    out=mnT.ap().rearrange("d (h n) -> d h n", n=NB)[:, :, n],
                    in_=mn_in[n].transpose([1, 0]), allow_slow_non_contiguous=True))
            DMA(semG0, lambda: g.dma_start(
                out=gsT[:], in_=gs_in.rearrange("o (oc q) -> q (o oc)", q=128),
                allow_slow_non_contiguous=True))
            DMA(semG0, lambda: g.dma_start(
                out=usT[:], in_=us_in.rearrange("o (oc q) -> q (o oc)", q=128),
                allow_slow_non_contiguous=True))
            rec["g0"] = DMA(semG0, lambda: g.dma_start(
                out=dsT[:], in_=ds_in.rearrange("o (pc q) -> q (o pc)", q=128),
                allow_slow_non_contiguous=True))

        @block.scalar
        def _(s):
            W(s, "g0")
            rec["silucb"] = COP(s, lambda: s.activation(silucf[:], siluc[:], AF.Silu))

        for k in range(C):
            @block.gpsimd
            def _(g, k=k):
                if k >= 1:
                    W(g, "modk%d" % (k - 1))
                rec["wmld%d" % k] = DMA(semWM[0], lambda: g.dma_start(
                    out=wmb[:], in_=wm_in[128 * k:128 * (k + 1), :]))

            @block.tensor
            def _(t, k=k):
                W(t, "wmld%d" % k)
                if k == 0:
                    W(t, "silucb")
                if k >= 2:
                    W(t, "macc%d" % (k - 2))
                bank = pC if k % 2 == 0 else pD
                last = None
                for oc in range(96):
                    last = t.matmul(
                        bank[:, oc:oc + 1],
                        wmb[:, 128 * oc:128 * (oc + 1)],
                        silucf[:, k:k + 1], start=True, stop=True)
                last.then_inc(semT, 1)
                rec["modk%d" % k] = (semT, _n(semT, 1))

            @block.vector
            def _(v, k=k):
                W(v, "modk%d" % k)
                bank = pC if k % 2 == 0 else pD
                v.wait_ge(semV, _cur(semV))
                if k == 0:
                    rec["macc%d" % k] = VOP(v, lambda: v.tensor_copy(
                        out=modacc[:], in_=bank[:, 0:96]))
                else:
                    rec["macc%d" % k] = VOP(v, lambda: v.tensor_add(
                        modacc[:], modacc[:], bank[:, 0:96]))

        @block.vector
        def _(v):
            mview = modacc.ap().rearrange("p (j k) -> p j k", k=16)
            gview = modacc.ap().rearrange("p (j k) -> p j k", k=16)[:, 2:6:3, :]
            VOP(v, lambda: v.tensor_scalar(
                mview[:, 1:5:3, :], mview[:, 1:5:3, :], 1.0, None, ALU.add))
            gv2 = gagb.ap().rearrange("p (j k) -> p j k", k=16)
            VOP(v, lambda: v.tensor_scalar(gv2, gview, 1e30, 1.0, ALU.mult, ALU.add))
            VOP(v, lambda: v.tensor_scalar(gagb[:], gagb[:], -1.0, 1.0, ALU.max, ALU.min))
            VOP(v, lambda: v.tensor_scalar(
                gtmp.ap().rearrange("p (j k) -> p j k", k=16), gview, -1.0, None, ALU.mult))
            VOP(v, lambda: v.tensor_tensor(
                gtmp.ap().rearrange("p (j k) -> p j k", k=16), gview,
                gtmp.ap().rearrange("p (j k) -> p j k", k=16), ALU.max))
            VOP(v, lambda: v.tensor_scalar_max(gtmp[:], gtmp[:], 0.25))
            rec["gagb"] = VOP(v, lambda: v.tensor_mul(gagb[:], gagb[:], gtmp[:]))
            W(v, "g0")
            rec["gds"] = VOP(v, lambda: v.tensor_mul(gds[:], dsT[:], gagb[:, 16:32]))
            rec["asum"] = VOP(v, lambda: v.reduce_sum(
                asum[:], mnT.ap().rearrange("d (h n) -> d n h", n=NB), axis=AX.X))
            VOP(v, lambda: v.reduce_max(bwR[:], bwS[:], axis=AX.X))
            rec["bwsub"] = VOP(v, lambda: v.tensor_scalar(
                bwE[:], bwS[:], bwR[:], None, ALU.subtract))

        @block.scalar
        def _(s):
            W(s, "bwsub")
            rec["bwexp"] = COP(s, lambda: s.activation(bwE[:], bwE[:], AF.Exp))

        @block.vector
        def _(v):
            W(v, "bwexp")
            VOP(v, lambda: v.reduce_sum(bwR[:], bwE[:], axis=AX.X))
            VOP(v, lambda: v.reciprocal(bwR[:], bwR[:]))
            rec["bwsm"] = VOP(v, lambda: v.tensor_scalar(
                bwE[:], bwE[:], bwR[:], None, ALU.mult))

        @block.tensor
        def _(t):
            W(t, "gds")
            t.transpose(pA[0:96, 0:128], modacc[:], identb[:]).then_inc(semT, 1)
            rec["modtr"] = (semT, _n(semT, 1))
            W(t, "bwsm")
            t.transpose(pB[0:4, 0:16], bwE[0:16, 0:4], identb[0:16, 0:16]).then_inc(semT, 1)
            rec["wdtr"] = (semT, _n(semT, 1))
            W(t, "asum")
            t.matmul(pB[0:4, 16:17], asum[:], onesb[:], start=True, stop=True).then_inc(semT, 1)
            rec["actmm"] = (semT, _n(semT, 1))

        @block.vector
        def _(v):
            W(v, "actmm")
            rec["modRc"] = VOP(v, lambda: v.tensor_copy(out=modR[:], in_=pA[0:96, 0:128]))
            rec["wdTc"] = VOP(v, lambda: v.tensor_copy(out=wdT[:], in_=pB[0:4, 0:16]))
            VOP(v, lambda: v.tensor_scalar(
                activ[:], pB[0:4, 16:17], ATT_EPS, 1e30, ALU.subtract, ALU.mult))
            rec["activ"] = VOP(v, lambda: v.tensor_scalar(
                activ[:], activ[:], 1.0, 0.0, ALU.add, ALU.max))
            rec["activ"] = VOP(v, lambda: v.tensor_scalar_min(activ[:], activ[:], 1.0))

        @block.scalar
        def _(s):
            W(s, "modRc")
            DMA(semA, lambda: s.dma_start(out=modscr, in_=modR[:]))
            s.wait_ge(semA, _cur(semA))
            DMA(semA, lambda: s.dma_start(out=lns1[:], in_=modscr[0:16, :].rearrange(
                "r e -> (r e)").unsqueeze(0).to_broadcast([128, D])))
            rec["ln1tiles"] = DMA(semA, lambda: s.dma_start(
                out=lnc1[:], in_=modscr[16:32, :].rearrange(
                    "r e -> (r e)").unsqueeze(0).to_broadcast([128, D])))

        emit_ln("ln1", lnc1, lns1, None, None)

        if upto == "mod":
            emit_tap(["activ", "ln1tr%d" % (TI - 1)],
                     dict(modacc=modacc[:], gagb=gagb[:], wdT=wdT[:], activ=activ[:],
                          h1T=hT[:], X=Xs[:]))
            _st0.close()
            return nc

        _st0.close()

        if upto == "h1":
            emit_tap(["ln1tr%d" % (TI - 1)], dict(h1T=hT[:], X=Xs[:]))
            return nc

        _st1 = contextlib.ExitStack()
        ec1 = _st1.enter_context
        sqc = ec1(nc.sbuf_tensor("sqc", [128, NSLOT * T], BF16))
        wsraw = ec1(nc.sbuf_tensor("wsraw", [128, 2 * D], BF16))
        wstr = ec1(nc.sbuf_tensor("wstr", [128, 2 * D], BF16))
        mem4 = ec1(nc.sbuf_tensor("mem4", [128, 4 * 512], BF16))
        cwb = ec1(nc.sbuf_tensor("cwb", [128, NB * T], BF16))
        cwbf = ec1(nc.sbuf_tensor("cwbf", [4, T], BF16))
        nrme = ec1(nc.sbuf_tensor("nrme", [4, T], BF16))
        nrmr = ec1(nc.sbuf_tensor("nrmr", [4, T], BF16))
        mbuf = ec1(nc.sbuf_tensor("mbuf", [128, T], BF16))
        rbuf = ec1(nc.sbuf_tensor("rbuf", [128, T], BF16))
        ebuf = ec1(nc.sbuf_tensor("ebuf", [128, T], BF16))
        sqs4 = ec1(nc.sbuf_tensor("sqs4", [128, NB * T], BF16))
        outTe = ec1(nc.sbuf_tensor("outTe", [128, 2 * T], BF16))
        otok = ec1(nc.sbuf_tensor("otok", [128, 2 * T], BF16))

        def sqslot(h):
            return sqc[:, T * (16 + h % NSQ):T * (17 + h % NSQ)]

        def combslot(j):
            return sqc[:, T * j:T * (j + 1)]

        @block.gpsimd
        def _(g):
            W(g, "actmm")
            W(g, "activ")
            W(g, "bwexp")
            for h in range(2):
                rec["wqld%d" % h] = DMA(semWQ[h % 2], lambda h=h: g.dma_start(
                    out=wsraw[:, D * (h % 2):D * (h % 2 + 1)],
                    in_=wq_in[128 * h:128 * (h + 1), :]))

        for u in range(H + 4):
            @block.vector
            def _(v, u=u):
                if 4 <= u and u - 4 < H:
                    h = u - 4
                    W(v, "tY%d" % h)
                    rec["vcomb%d" % h] = VOP(v, lambda: v.tensor_copy(
                        out=combslot(h), in_=pD[:, :]))
                if 3 <= u and u - 3 < H:
                    h = u - 3
                    W(v, "acw%d" % h)
                    r = None
                    for n in range(NB):
                        r = VOP(v, lambda n=n: v.tensor_mul(
                            sqs4[:, T * n:T * (n + 1)], sqslot(h),
                            cwb[:, T * n:T * (n + 1)]))
                    rec["vsqs%d" % h] = r
                if 1 <= u and u - 1 < H:
                    h = u - 1
                    W(v, "tq%d" % h)
                    if h >= 1:
                        W(v, "cexp%d" % (h - 1))
                    qp = pA if h % 2 == 0 else pB
                    rec["vmin%d" % h] = VOP(v, lambda: v.tensor_scalar_min(
                        mbuf[:], qp[:, :], 0.0))

            @block.scalar
            def _(s, u=u):
                if 1 <= u and u - 1 < H:
                    h = u - 1
                    W(s, "tq%d" % h)
                    if h >= 1:
                        W(s, "vsq%d" % (h - 1))
                    qp = pA if h % 2 == 0 else pB
                    COP(s, lambda: s.activation(rbuf[:], qp[:, :], AF.Relu))
                    W(s, "vmin%d" % h)
                    rec["cexp%d" % h] = COP(s, lambda: s.activation(ebuf[:], mbuf[:], AF.Exp))

            @block.vector
            def _(v, u=u):
                if 1 <= u and u - 1 < H:
                    h = u - 1
                    W(v, "cexp%d" % h)
                    if h == 0:
                        W(v, "modk%d" % (C - 1))
                    rec["vsq%d" % h] = VOP(v, lambda: v.tensor_add(
                        sqslot(h), ebuf[:], rbuf[:]))
                if 2 <= u and u - 2 < H:
                    h = u - 2
                    W(v, "tn%d" % h)
                    rec["vnmax%d" % h] = VOP(v, lambda: v.tensor_scalar_max(
                        nrme[:], pC[0:4, :], ATT_EPS))
                    with nc.allow_low_precision(reason="bank-norm recip at bf16"):
                        VOP(v, lambda: v.reciprocal(nrmr[:], nrme[:]))
                    rec["vcw%d" % h] = VOP(v, lambda: v.tensor_scalar(
                        cwbf[:], nrmr[:], wdT[:, h:h + 1], activ[:], ALU.mult, ALU.mult))

            @block.scalar
            def _(s, u=u):
                if 2 <= u and u - 2 < H:
                    h = u - 2
                    W(s, "vcw%d" % h)
                    DMA(semA, lambda: s.dma_start(
                        out=cwscr[h:h + 1, :].rearrange("o (n t) -> (o n) t", n=NB),
                        in_=cwbf[:]))
                    s.wait_ge(semA, _cur(semA))
                    if h >= 1:
                        W(s, "vsqs%d" % (h - 1))
                    rec["acw%d" % h] = DMA(semA, lambda: s.dma_start(
                        out=cwb[:], in_=cwscr[h:h + 1, :].to_broadcast([128, NB * T])))

            @block.sync
            def _(y, u=u):
                if u < H:
                    h = u
                    W(y, "wqld%d" % h)
                    if h >= 2:
                        W(y, "tq%d" % (h - 2))
                    rec["wqtr%d" % h] = DMA(semTR[h % 2], lambda: y.dma_start_transpose(
                        wstr[:, D * (h % 2):D * (h % 2 + 1)].rearrange("p (k q) -> p k q", k=C),
                        wsraw[:, D * (h % 2):D * (h % 2 + 1)].rearrange("p (k q) -> p k q", k=C)))

            @block.tensor
            def _(t, u=u):
                if u < H:
                    h = u
                    W(t, "wqtr%d" % h)
                    if h == 0:
                        W(t, "ln1tr%d" % (TI - 1))
                        W(t, "activ")
                    if h >= 2:
                        W(t, "vmin%d" % (h - 2))
                        W(t, "cexp%d" % (h - 2))
                    qp = pA if h % 2 == 0 else pB
                    items = []
                    for k in range(C):
                        for hf in range(2):
                            items.append((
                                qp[:, 512 * hf:512 * (hf + 1)],
                                wstr[:, D * (h % 2) + 128 * k:D * (h % 2) + 128 * (k + 1)],
                                hTv[:, k, 512 * hf:512 * (hf + 1)],
                                k == 0, k == C - 1))
                    rec["tq%d" % h] = MMGROUP(t, items)
                if 1 <= u <= H:
                    h = u - 1
                    W(t, "vsq%d" % h)
                    if h >= 1:
                        W(t, "vnmax%d" % (h - 1))
                    items = [(pC[0:4, 512 * hf:512 * (hf + 1)],
                              mnT[:, 4 * h:4 * (h + 1)],
                              sqslot(h)[:, 512 * hf:512 * (hf + 1)],
                              True, True) for hf in range(2)]
                    rec["tn%d" % h] = MMGROUP(t, items)
                if 3 <= u and u - 3 < H:
                    h = u - 3
                    W(t, "vsqs%d" % h)
                    t.wait_ge(semME, 16 * (h + 1))
                    if h >= 1:
                        W(t, "vcomb%d" % (h - 1))
                    items = []
                    for n in range(NB):
                        for hf in range(2):
                            items.append((
                                pD[:, 512 * hf:512 * (hf + 1)],
                                mem4[:, 512 * (h % 4) + 128 * n:512 * (h % 4) + 128 * (n + 1)],
                                sqs4[:, T * n + 512 * hf:T * n + 512 * (hf + 1)],
                                n == 0, n == NB - 1))
                    rec["tY%d" % h] = MMGROUP(t, items)

            @block.gpsimd
            def _(g, u=u):
                if u + 2 < H:
                    h = u + 2
                    W(g, "wqtr%d" % (h - 2))
                    rec["wqld%d" % h] = DMA(semWQ[h % 2], lambda: g.dma_start(
                        out=wsraw[:, D * (h % 2):D * (h % 2 + 1)],
                        in_=wq_in[128 * h:128 * (h + 1), :]))
                if u < H:
                    h = u
                    if h >= 4:
                        W(g, "tY%d" % (h - 4))
                    rec["memld%d" % h] = SDMA(g, semME, lambda: g.dma_start(
                        out=mem4[:, 512 * (h % 4):512 * (h % 4 + 1)].rearrange(
                            "d (n e) -> d n e", n=NB),
                        in_=mem_in[:, h].transpose([1, 0, 2])))

        if upto == "attn":
            emit_tap(["vcomb%d" % (H - 1)], dict(comb=sqc[:, 0:16 * T], X=Xs[:]))
            _st1.close()
            return nc

        for u in range(H + 2):
            @block.vector
            def _(v, u=u):
                if 1 <= u <= H:
                    p = u - 1
                    W(v, "two%d" % p)
                    wp = pA if p % 2 == 0 else pB
                    rec["vwoe%d" % p] = VOP(v, lambda: v.tensor_scalar(
                        outTe[:, T * (p % 2):T * (p % 2 + 1)], wp[:, :],
                        gagb[:, p:p + 1], None, ALU.mult))
                if u >= 2:
                    p = u - 2
                    W(v, "wotr2%d" % p)
                    rec["vxadd%d" % p] = VOP(v, lambda: v.tensor_add(
                        Xv[:, :, 128 * p:128 * (p + 1)],
                        Xv[:, :, 128 * p:128 * (p + 1)],
                        otok[:, T * (p % 2):T * (p % 2 + 1)].rearrange(
                            "p (ti q) -> p ti q", ti=TI)))

            @block.gpsimd
            def _(g, u=u):
                if u < H:
                    p = u
                    if p < 2:
                        W(g, "wqtr%d" % (H - 2 + p))
                    else:
                        W(g, "wotr%d" % (p - 2))
                    rec["wold%d" % p] = DMA(semWQ[p % 2], lambda: g.dma_start(
                        out=wsraw[:, D * (p % 2):D * (p % 2 + 1)],
                        in_=wo_in[128 * p:128 * (p + 1), :]))

            @block.sync
            def _(y, u=u):
                if 1 <= u <= H:
                    p = u - 1
                    W(y, "vwoe%d" % p)
                    if p >= 2:
                        W(y, "vxadd%d" % (p - 2))
                    rec["wotr2%d" % p] = DMA(semOT[p % 2], lambda: y.dma_start_transpose(
                        otok[:, T * (p % 2):T * (p % 2 + 1)].rearrange("p (ti q) -> p ti q", ti=TI),
                        outTe[:, T * (p % 2):T * (p % 2 + 1)].rearrange("p (ti q) -> p ti q", ti=TI)))
                if u < H:
                    p = u
                    W(y, "wold%d" % p)
                    if p >= 2:
                        W(y, "two%d" % (p - 2))
                    rec["wotr%d" % p] = DMA(semTR[p % 2], lambda: y.dma_start_transpose(
                        wstr[:, D * (p % 2):D * (p % 2 + 1)].rearrange("p (k q) -> p k q", k=C),
                        wsraw[:, D * (p % 2):D * (p % 2 + 1)].rearrange("p (k q) -> p k q", k=C)))

            @block.tensor
            def _(t, u=u):
                if u < H:
                    p = u
                    W(t, "wotr%d" % p)
                    if p == 0:
                        W(t, "vcomb%d" % (H - 1))
                        W(t, "vmin%d" % (H - 1))
                        W(t, "cexp%d" % (H - 1))
                    if p >= 2:
                        W(t, "vwoe%d" % (p - 2))
                    wp = pA if p % 2 == 0 else pB
                    items = []
                    for j in range(C):
                        for hf in range(2):
                            items.append((
                                wp[:, 512 * hf:512 * (hf + 1)],
                                wstr[:, D * (p % 2) + 128 * j:D * (p % 2) + 128 * (j + 1)],
                                combslot(j)[:, 512 * hf:512 * (hf + 1)],
                                j == 0, j == C - 1))
                    rec["two%d" % p] = MMGROUP(t, items)

        _st1.close()

        if upto == "x1":
            emit_tap(["vxadd%d" % (H - 1)], dict(X=Xs[:]))
            return nc

        _st2 = contextlib.ExitStack()
        ec2 = _st2.enter_context
        lnc2 = ec2(nc.sbuf_tensor("lnc2", [128, D], BF16))
        lns2 = ec2(nc.sbuf_tensor("lns2", [128, D], BF16))
        fuse = ec2(nc.sbuf_tensor("fuse", [128, OC2 * T], BF16))
        g8 = ec2(nc.sbuf_tensor("g8", [128, 2 * D], U8))
        u8 = ec2(nc.sbuf_tensor("u8", [128, 2 * D], U8))
        d8 = ec2(nc.sbuf_tensor("d8", [128, 2 * OC2 * 128], U8))
        sgb = ec2(nc.sbuf_tensor("sg", [128, 2 * T], BF16))
        outTe2 = ec2(nc.sbuf_tensor("outTe2", [128, 2 * T], BF16))
        otok2 = ec2(nc.sbuf_tensor("otok2", [128, 2 * T], BF16))

        @block.scalar
        def _(s):
            W(s, "vxadd%d" % (H - 1))
            DMA(semA, lambda: s.dma_start(out=lns2[:], in_=modscr[48:64, :].rearrange(
                "r e -> (r e)").unsqueeze(0).to_broadcast([128, D])))
            rec["ln2tiles"] = DMA(semA, lambda: s.dma_start(
                out=lnc2[:], in_=modscr[64:80, :].rearrange(
                    "r e -> (r e)").unsqueeze(0).to_broadcast([128, D])))

        emit_ln("ln2", lnc2, lns2, "ln1", "vxadd%d" % (H - 1))

        if upto == "h2":
            emit_tap(["ln2tr%d" % (TI - 1)], dict(h2T=hT[:], X=Xs[:]))
            _st2.close()
            return nc

        for oh in range(2):
            for u in range(OC2 + 1):
                @block.scalar
                def _(s, u=u, oh=oh):
                    if 1 <= u:
                        oc = u - 1
                        ocg = OC2 * oh + oc
                        W(s, "tg_%d_%d" % (oh, oc))
                        rec["csil_%d_%d" % (oh, oc)] = COP(s, lambda: s.activation(
                            sgb[:, T * (oc % 2):T * (oc % 2 + 1)], pA[:, :], AF.Silu,
                            scale=gsT[:, ocg:ocg + 1]))

                @block.vector
                def _(v, u=u, oh=oh):
                    if 1 <= u:
                        oc = u - 1
                        ocg = OC2 * oh + oc
                        W(v, "tu_%d_%d" % (oh, oc))
                        W(v, "csil_%d_%d" % (oh, oc))
                        if oh == 1 and oc == 0:
                            W(v, "td_0_%d" % (H - 1))
                        rec["vfu_%d_%d" % (oh, oc)] = VOP(v, lambda: v.scalar_tensor_tensor(
                            fuse[:, T * oc:T * (oc + 1)], pB[:, :], usT[:, ocg:ocg + 1],
                            sgb[:, T * (oc % 2):T * (oc % 2 + 1)], ALU.mult, ALU.mult))

                @block.gpsimd
                def _(g, u=u, oh=oh):
                    if u < OC2:
                        oc = u
                        ocg = OC2 * oh + oc
                        if ocg == 0:
                            W(g, "two%d" % (H - 1))
                            W(g, "vxadd%d" % (H - 1))
                        if ocg >= 2:
                            po, pc_ = (ocg - 2) // OC2, (ocg - 2) % OC2
                            W(g, "tg_%d_%d" % (po, pc_))
                        rec["g8_%d_%d" % (oh, oc)] = SDMA(g, semG8, lambda: g.dma_start(
                            out=g8[:, D * (oc % 2):D * (oc % 2 + 1)].bitcast(F8).rearrange(
                                "q (k e) -> q k e", k=C),
                            in_=gw_in[:, 128 * ocg:128 * (ocg + 1)].rearrange(
                                "(k q) e -> q k e", q=128)))
                        if ocg >= 2:
                            po, pc_ = (ocg - 2) // OC2, (ocg - 2) % OC2
                            W(g, "tu_%d_%d" % (po, pc_))
                        rec["u8_%d_%d" % (oh, oc)] = SDMA(g, semU8, lambda: g.dma_start(
                            out=u8[:, D * (oc % 2):D * (oc % 2 + 1)].bitcast(F8).rearrange(
                                "q (k e) -> q k e", k=C),
                            in_=uw_in[:, 128 * ocg:128 * (ocg + 1)].rearrange(
                                "(k q) e -> q k e", q=128)))

                @block.tensor
                def _(t, u=u, oh=oh):
                    if u < OC2:
                        oc = u
                        W(t, "g8_%d_%d" % (oh, oc))
                        if oh + oc == 0:
                            W(t, "ln2tr%d" % (TI - 1))
                            W(t, "vwoe%d" % (H - 1))
                        if oc >= 1 or oh == 1:
                            po, pc_ = (oh, oc - 1) if oc >= 1 else (0, OC2 - 1)
                            W(t, "csil_%d_%d" % (po, pc_))
                        items = []
                        for k in range(C):
                            for hf in range(2):
                                items.append((
                                    pA[:, 512 * hf:512 * (hf + 1)],
                                    g8[:, D * (oc % 2) + 128 * k:D * (oc % 2) + 128 * (k + 1)].bitcast(F8),
                                    hTv[:, k, 512 * hf:512 * (hf + 1)],
                                    k == 0, k == C - 1))
                        rec["tg_%d_%d" % (oh, oc)] = MMGROUP(t, items)
                        W(t, "u8_%d_%d" % (oh, oc))
                        if oc >= 1 or oh == 1:
                            po, pc_ = (oh, oc - 1) if oc >= 1 else (0, OC2 - 1)
                            W(t, "vfu_%d_%d" % (po, pc_))
                        items = []
                        for k in range(C):
                            for hf in range(2):
                                items.append((
                                    pB[:, 512 * hf:512 * (hf + 1)],
                                    u8[:, D * (oc % 2) + 128 * k:D * (oc % 2) + 128 * (k + 1)].bitcast(F8),
                                    hTv[:, k, 512 * hf:512 * (hf + 1)],
                                    k == 0, k == C - 1))
                        rec["tu_%d_%d" % (oh, oc)] = MMGROUP(t, items)

            for u in range(H + 2):
                @block.vector
                def _(v, u=u, oh=oh):
                    if 1 <= u <= H:
                        p = u - 1
                        W(v, "td_%d_%d" % (oh, p))
                        dp = pC if p % 2 == 0 else pD
                        rec["vde_%d_%d" % (oh, p)] = VOP(v, lambda: v.tensor_scalar(
                            outTe2[:, T * (p % 2):T * (p % 2 + 1)], dp[:, :],
                            gds[:, p:p + 1], None, ALU.mult))
                    if u >= 2:
                        p = u - 2
                        W(v, "dtr_%d_%d" % (oh, p))
                        rec["vda_%d_%d" % (oh, p)] = VOP(v, lambda: v.tensor_add(
                            Xv[:, :, 128 * p:128 * (p + 1)],
                            Xv[:, :, 128 * p:128 * (p + 1)],
                            otok2[:, T * (p % 2):T * (p % 2 + 1)].rearrange(
                                "p (ti q) -> p ti q", ti=TI)))

                @block.gpsimd
                def _(g, u=u, oh=oh):
                    if u < H:
                        p = u
                        if oh == 1 or p >= 2:
                            po, pp = (oh, p - 2) if p >= 2 else (0, H - 2 + p)
                            W(g, "td_%d_%d" % (po, pp))
                        rec["d8_%d_%d" % (oh, p)] = SDMA(g, semD8, lambda: g.dma_start(
                            out=d8[:, OC2 * 128 * (p % 2):OC2 * 128 * (p % 2 + 1)].bitcast(F8).rearrange(
                                "q (oc e) -> q oc e", oc=OC2),
                            in_=dw_in[3072 * oh:3072 * (oh + 1),
                                      128 * p:128 * (p + 1)].rearrange(
                                "(oc q) e -> q oc e", q=128)))

                @block.sync
                def _(y, u=u, oh=oh):
                    if 1 <= u <= H:
                        p = u - 1
                        W(y, "vde_%d_%d" % (oh, p))
                        if p >= 2:
                            W(y, "vda_%d_%d" % (oh, p - 2))
                        elif oh == 1:
                            W(y, "vda_0_%d" % (H - 2 + p))
                        rec["dtr_%d_%d" % (oh, p)] = DMA(semOT[p % 2], lambda: y.dma_start_transpose(
                            otok2[:, T * (p % 2):T * (p % 2 + 1)].rearrange("p (ti q) -> p ti q", ti=TI),
                            outTe2[:, T * (p % 2):T * (p % 2 + 1)].rearrange("p (ti q) -> p ti q", ti=TI)))
                    if u >= 2 and oh == 1:
                        p = u - 2
                        W(y, "vda_1_%d" % p)
                        rec["st%d" % p] = DMA(semST, lambda: y.dma_start(
                            out=out_d.rearrange("(ti q) d -> q ti d", q=128)[:, :, 128 * p:128 * (p + 1)],
                            in_=Xv[:, :, 128 * p:128 * (p + 1)]))

                @block.tensor
                def _(t, u=u, oh=oh):
                    if u < H:
                        p = u
                        W(t, "d8_%d_%d" % (oh, p))
                        if p == 0:
                            W(t, "vfu_%d_%d" % (oh, OC2 - 1))
                            if oh == 0:
                                W(t, "vnmax%d" % (H - 1))
                                W(t, "vcomb%d" % (H - 1))
                        if p >= 2:
                            W(t, "vde_%d_%d" % (oh, p - 2))
                        elif oh == 1:
                            W(t, "vde_0_%d" % (H - 2 + p))
                        dp = pC if p % 2 == 0 else pD
                        items = []
                        for oc in range(OC2):
                            for hf in range(2):
                                items.append((
                                    dp[:, 512 * hf:512 * (hf + 1)],
                                    d8[:, OC2 * 128 * (p % 2) + 128 * oc:
                                       OC2 * 128 * (p % 2) + 128 * (oc + 1)].bitcast(F8),
                                    fuse[:, T * oc + 512 * hf:T * oc + 512 * (hf + 1)],
                                    oc == 0, oc == OC2 - 1))
                        rec["td_%d_%d" % (oh, p)] = MMGROUP(t, items)

        _st2.close()

        @block.sync
        def _(y):
            y.wait_ge(semST, 16 * H)

    return nc


def _consts():
    cst = np.zeros((128, 129), np.float32)
    cst[:, 0:128] = np.eye(128, dtype=np.float32)
    cst[:, 128] = 1.0
    return cst


def kernel(L0_attn_hidden_states, L0_attn_Wq, L0_attn_Wk, L0_attn_Wv, L0_attn_Wo,
           L0_attn_bank_weights, L0_attn_memories, L0_attn_memory_norms,
           L0_ffn_gate_w, L0_ffn_up_w, L0_ffn_down_w, L0_ffn_gate_scale,
           L0_ffn_up_scale, L0_ffn_down_scale, L0_c, L0_wmod):
    nc = build_nc()
    asnp = lambda a: np.ascontiguousarray(np.asarray(a, dtype=np.float32))
    x = asnp(L0_attn_hidden_states)
    shared = {
        "wq": asnp(L0_attn_Wq), "wo": asnp(L0_attn_Wo),
        "bw": asnp(L0_attn_bank_weights), "mem": asnp(L0_attn_memories),
        "mn": asnp(L0_attn_memory_norms), "gw": asnp(L0_ffn_gate_w),
        "uw": asnp(L0_ffn_up_w), "dw": asnp(L0_ffn_down_w),
        "gs": asnp(L0_ffn_gate_scale).reshape(1, FI),
        "us": asnp(L0_ffn_up_scale).reshape(1, FI),
        "ds": asnp(L0_ffn_down_scale).reshape(1, D),
        "wm": asnp(L0_wmod), "cst": _consts(),
    }
    cfull = asnp(L0_c)
    in_maps = []
    for c in range(8):
        b, th = c // 2, c % 2
        m = dict(shared)
        m["x"] = np.ascontiguousarray(x[b, T * th:T * (th + 1), :])
        m["cb"] = np.ascontiguousarray(cfull[b:b + 1, :])
        in_maps.append(m)
    res = run_bass_kernel_spmd(nc, in_maps, core_ids=list(range(8)))
    out = np.empty((4, 2048, D), np.float32)
    for c in range(8):
        b, th = c // 2, c % 2
        out[b, T * th:T * (th + 1), :] = res.results[c]["out"]
    return out


# version 07 (final)
# speedup vs baseline: 1.3392x; 1.0473x over previous
import contextlib
import numpy as np
import concourse.bass as bass
import concourse.mybir as mybir
from concourse.bass_utils import run_bass_kernel_spmd

F32 = mybir.dt.float32
BF16 = mybir.dt.bfloat16
F8 = mybir.dt.float8e4
U8 = mybir.dt.uint8
AF = mybir.ActivationFunctionType
ALU = mybir.AluOpType
AX = mybir.AxisListType

D = 2048
T = 1024
C = 16
TI = 8
H = 16
NB = 4
FI = 6144
OC2 = 24
NSQ = 3
NSLOT = 16 + NSQ
LN_EPS = 1e-5
ATT_EPS = 1e-6


def build_nc(upto="all", taps=()):
    nc = bass.Bass("TRN2", target_bir_lowering=False)

    x_in = nc.dram_tensor("x", [T, D], F32, kind="ExternalInput").ap()
    cb_in = nc.dram_tensor("cb", [1, D], F32, kind="ExternalInput").ap()
    wq_in = nc.dram_tensor("wq", [D, D], F32, kind="ExternalInput").ap()
    wo_in = nc.dram_tensor("wo", [D, D], F32, kind="ExternalInput").ap()
    bw_in = nc.dram_tensor("bw", [H, NB], F32, kind="ExternalInput").ap()
    mem_in = nc.dram_tensor("mem", [NB, H, 128, 128], F32, kind="ExternalInput").ap()
    mn_in = nc.dram_tensor("mn", [NB, H, 128], F32, kind="ExternalInput").ap()
    gw_in = nc.dram_tensor("gw", [D, FI], F32, kind="ExternalInput").ap()
    uw_in = nc.dram_tensor("uw", [D, FI], F32, kind="ExternalInput").ap()
    dw_in = nc.dram_tensor("dw", [FI, D], F32, kind="ExternalInput").ap()
    gs_in = nc.dram_tensor("gs", [1, FI], F32, kind="ExternalInput").ap()
    us_in = nc.dram_tensor("us", [1, FI], F32, kind="ExternalInput").ap()
    ds_in = nc.dram_tensor("ds", [1, D], F32, kind="ExternalInput").ap()
    wm_in = nc.dram_tensor("wm", [D, 6 * D], F32, kind="ExternalInput").ap()
    cst_in = nc.dram_tensor("cst", [128, 129], F32, kind="ExternalInput").ap()
    out_d = nc.dram_tensor("out", [T, D], F32, kind="ExternalOutput").ap()
    modscr = nc.dram_tensor("modscr", [96, 128], BF16, kind="Internal").ap()
    cwscr = nc.dram_tensor("cwscr", [H, NB * T], BF16, kind="Internal").ap()

    tap_d = {}
    for name, shape, dt in taps:
        tap_d[name] = nc.dram_tensor("tap_" + name, list(shape), dt, kind="ExternalOutput").ap()

    cnt = {}
    rec = {}

    with contextlib.ExitStack() as _st:
        ec = _st.enter_context
        semA = ec(nc.semaphore("semA"))
        semT = ec(nc.semaphore("semT"))
        semV = ec(nc.semaphore("semV"))
        semC = ec(nc.semaphore("semC"))
        semX = ec(nc.semaphore("semX"))
        semLN = ec(nc.semaphore("semLN"))
        semTR = [ec(nc.semaphore("semTR0")), ec(nc.semaphore("semTR1"))]
        semOT = [ec(nc.semaphore("semOT0")), ec(nc.semaphore("semOT1"))]
        semST = ec(nc.semaphore("semST"))
        semG0 = ec(nc.semaphore("semG0"))
        semWM = [ec(nc.semaphore("semWM0")), ec(nc.semaphore("semWM1"))]
        semWQ = [ec(nc.semaphore("semWQ0")), ec(nc.semaphore("semWQ1"))]
        semME = ec(nc.semaphore("semME"))
        semG8 = ec(nc.semaphore("semG8"))
        semU8 = ec(nc.semaphore("semU8"))
        semD8 = ec(nc.semaphore("semD8"))
        semTP = ec(nc.semaphore("semTP"))
        Xs = ec(nc.sbuf_tensor("X", [128, TI * D], F32))
        hT = ec(nc.sbuf_tensor("hT", [128, C * T], BF16))
        htok = ec(nc.sbuf_tensor("htok", [128, D], BF16))
        httmp = ec(nc.sbuf_tensor("httmp", [128, D], BF16))
        httmp2 = ec(nc.sbuf_tensor("httmp2", [128, D], BF16))
        gagb = ec(nc.sbuf_tensor("gagb", [128, 32], F32))
        siluc = ec(nc.sbuf_tensor("siluc", [128, 16], F32))
        silucb = ec(nc.sbuf_tensor("silucb", [128, 16], BF16))
        stats6 = ec(nc.sbuf_tensor("stats6", [128, 24], F32))
        stats2 = ec(nc.sbuf_tensor("stats2", [128, 2], F32))
        lnsm = ec(nc.sbuf_tensor("lnsm", [128, 2], F32))
        onesb = ec(nc.sbuf_tensor("ones", [128, 1], F32))
        identb = ec(nc.sbuf_tensor("ident", [128, 128], F32))
        gsT = ec(nc.sbuf_tensor("gsT", [128, 48], F32))
        usT = ec(nc.sbuf_tensor("usT", [128, 48], F32))
        dsT = ec(nc.sbuf_tensor("dsT", [128, 16], F32))
        gds = ec(nc.sbuf_tensor("gds", [128, 16], F32))
        mnT = ec(nc.sbuf_tensor("mnT", [128, H * NB], BF16))
        wdT = ec(nc.sbuf_tensor("wdT", [4, 16], F32))
        activ = ec(nc.sbuf_tensor("activ", [4, 1], F32))
        pA = ec(nc.psum_tensor("pA", [128, 1024], F32))
        pB = ec(nc.psum_tensor("pB", [128, 1024], F32))
        pC = ec(nc.psum_tensor("pC", [128, 1024], F32))
        pD = ec(nc.psum_tensor("pD", [128, 1024], F32))
        block = ec(nc.Block())

        Xv = Xs.ap().rearrange("p (ti d) -> p ti d", ti=TI)
        hTv = hT.ap().rearrange("p (j t) -> p j t", j=C)

        def _n(sem, k):
            key = id(sem)
            cnt[key] = cnt.get(key, 0) + k
            return cnt[key]

        def _cur(sem):
            return cnt.get(id(sem), 0)

        def VOP(v, mk):
            v.wait_ge(semV, _cur(semV))
            mk().then_inc(semV, 1)
            return (semV, _n(semV, 1))

        def COP(s, mk):
            s.wait_ge(semC, _cur(semC))
            mk().then_inc(semC, 1)
            return (semC, _n(semC, 1))

        def DMA(sem, mk):
            mk().then_inc(sem, 16)
            return (sem, _n(sem, 16))

        def SDMA(g, sem, mk):
            g.wait_ge(sem, _cur(sem))
            return DMA(sem, mk)

        def MMGROUP(t, items):
            last = None
            for (o, l, r, st, sp) in items:
                last = t.matmul(o, l, r, start=st, stop=sp)
            last.then_inc(semT, 1)
            return (semT, _n(semT, 1))

        def W(eng, key):
            sem, val = rec[key]
            eng.wait_ge(sem, val)

        def emit_tap(gatekeys, srcs):
            @block.gpsimd
            def _(g):
                for kk in gatekeys:
                    W(g, kk)
                for name, shape, dt in taps:
                    if name in srcs:
                        DMA(semTP, lambda name=name: g.dma_start(
                            out=tap_d[name], in_=srcs[name]))
                g.wait_ge(semTP, _cur(semTP))

        def emit_ln(tag, lnc, lns, prev_tag, xgate):
            for ti in range(TI):
                @block.vector
                def _(v, ti=ti):
                    if ti == 0:
                        W(v, "x")
                        W(v, "%stiles" % tag)
                        if xgate is not None:
                            W(v, xgate)
                    xt = Xs[:, D * ti:D * (ti + 1)]
                    for gr in range(4):
                        VOP(v, lambda gr=gr: v.bn_stats(
                            stats6[:, 6 * gr:6 * gr + 6], xt[:, 512 * gr:512 * (gr + 1)]))
                    VOP(v, lambda: v.bn_aggr(stats2[:], stats6[:]))
                    rec["%seps%d" % (tag, ti)] = VOP(v, lambda: v.tensor_scalar(
                        lnsm[:, 0:1], stats2[:, 1:2], LN_EPS, None, ALU.add))

                @block.scalar
                def _(s, ti=ti):
                    W(s, "%seps%d" % (tag, ti))
                    rec["%ssq%d" % (tag, ti)] = COP(s, lambda: s.activation(
                        lnsm[:, 1:2], lnsm[:, 0:1], AF.Sqrt))

                @block.vector
                def _(v, ti=ti):
                    xt = Xs[:, D * ti:D * (ti + 1)]
                    W(v, "%ssq%d" % (tag, ti))
                    VOP(v, lambda: v.reciprocal(lnsm[:, 1:2], lnsm[:, 1:2]))
                    VOP(v, lambda: v.tensor_scalar(
                        httmp[:], xt, stats2[:, 0:1], lnsm[:, 1:2], ALU.subtract, ALU.mult))
                    VOP(v, lambda: v.tensor_mul(httmp2[:], httmp[:], lnc[:]))
                    if ti >= 1:
                        W(v, "%str%d" % (tag, ti - 1))
                    elif prev_tag is not None:
                        W(v, "%str%d" % (prev_tag, TI - 1))
                    rec["%sv%d" % (tag, ti)] = VOP(v, lambda: v.tensor_add(
                        htok[:], httmp2[:], lns[:]))

                @block.sync
                def _(y, ti=ti):
                    W(y, "%sv%d" % (tag, ti))
                    if tag == "ln2" and ti == 0:
                        W(y, "tq%d" % (H - 1))
                    rec["%str%d" % (tag, ti)] = DMA(semLN, lambda: y.dma_start_transpose(
                        hTv[:, :, 128 * ti:128 * (ti + 1)],
                        htok.ap().rearrange("p (j q) -> p j q", j=C)))

        _st0 = contextlib.ExitStack()
        ec0 = _st0.enter_context
        wmb = ec0(nc.sbuf_tensor("wmb", [128, 12 * D], BF16))
        silucf = ec0(nc.sbuf_tensor("silucf", [128, 16], BF16))
        bwS = ec0(nc.sbuf_tensor("bwS", [H, NB], F32))
        bwE = ec0(nc.sbuf_tensor("bwE", [H, NB], F32))
        bwR = ec0(nc.sbuf_tensor("bwR", [H, 1], F32))
        asum = ec0(nc.sbuf_tensor("asum", [128, 4], F32))
        modacc = ec0(nc.sbuf_tensor("modacc", [128, 96], F32))
        gtmp = ec0(nc.sbuf_tensor("gtmp", [128, 32], F32))
        modR = ec0(nc.sbuf_tensor("modR", [96, 128], BF16))
        lnc1 = ec0(nc.sbuf_tensor("lnc1", [128, D], BF16))
        lns1 = ec0(nc.sbuf_tensor("lns1", [128, D], BF16))

        @block.sync
        def _(y):
            rec["x"] = DMA(semX, lambda: y.dma_start(
                out=Xv, in_=x_in.rearrange("(ti p) d -> p ti d", p=128)))

        @block.gpsimd
        def _(g):
            DMA(semG0, lambda: g.dma_start(out=identb[:], in_=cst_in[:, 0:128]))
            DMA(semG0, lambda: g.dma_start(out=onesb[:], in_=cst_in[:, 128:129],
                                           allow_slow_non_contiguous=True))
            DMA(semG0, lambda: g.dma_start(
                out=siluc[:], in_=cb_in.rearrange("o (k p) -> p (o k)", p=128),
                allow_slow_non_contiguous=True))
            DMA(semG0, lambda: g.dma_start(out=bwS[:], in_=bw_in))
            for n in range(NB):
                DMA(semG0, lambda n=n: g.dma_start(
                    out=mnT.ap().rearrange("d (h n) -> d h n", n=NB)[:, :, n],
                    in_=mn_in[n].transpose([1, 0]), allow_slow_non_contiguous=True))
            DMA(semG0, lambda: g.dma_start(
                out=gsT[:], in_=gs_in.rearrange("o (oc q) -> q (o oc)", q=128),
                allow_slow_non_contiguous=True))
            DMA(semG0, lambda: g.dma_start(
                out=usT[:], in_=us_in.rearrange("o (oc q) -> q (o oc)", q=128),
                allow_slow_non_contiguous=True))
            rec["g0"] = DMA(semG0, lambda: g.dma_start(
                out=dsT[:], in_=ds_in.rearrange("o (pc q) -> q (o pc)", q=128),
                allow_slow_non_contiguous=True))

        @block.scalar
        def _(s):
            W(s, "g0")
            rec["silucb"] = COP(s, lambda: s.activation(silucf[:], siluc[:], AF.Silu))

        for k in range(C):
            @block.gpsimd
            def _(g, k=k):
                if k >= 2:
                    W(g, "modk%d" % (k - 2))
                rec["wmld%d" % k] = DMA(semWM[k % 2], lambda: g.dma_start(
                    out=wmb[:, 6 * D * (k % 2):6 * D * (k % 2 + 1)],
                    in_=wm_in[128 * k:128 * (k + 1), :]))

            @block.tensor
            def _(t, k=k):
                W(t, "wmld%d" % k)
                if k == 0:
                    W(t, "silucb")
                if k >= 2:
                    W(t, "macc%d" % (k - 2))
                bank = pC if k % 2 == 0 else pD
                last = None
                for oc in range(96):
                    last = t.matmul(
                        bank[:, oc:oc + 1],
                        wmb[:, 6 * D * (k % 2) + 128 * oc:6 * D * (k % 2) + 128 * (oc + 1)],
                        silucf[:, k:k + 1], start=True, stop=True)
                last.then_inc(semT, 1)
                rec["modk%d" % k] = (semT, _n(semT, 1))

            @block.vector
            def _(v, k=k):
                W(v, "modk%d" % k)
                bank = pC if k % 2 == 0 else pD
                v.wait_ge(semV, _cur(semV))
                if k == 0:
                    rec["macc%d" % k] = VOP(v, lambda: v.tensor_copy(
                        out=modacc[:], in_=bank[:, 0:96]))
                else:
                    rec["macc%d" % k] = VOP(v, lambda: v.tensor_add(
                        modacc[:], modacc[:], bank[:, 0:96]))

        @block.vector
        def _(v):
            mview = modacc.ap().rearrange("p (j k) -> p j k", k=16)
            gview = modacc.ap().rearrange("p (j k) -> p j k", k=16)[:, 2:6:3, :]
            VOP(v, lambda: v.tensor_scalar(
                mview[:, 1:5:3, :], mview[:, 1:5:3, :], 1.0, None, ALU.add))
            gv2 = gagb.ap().rearrange("p (j k) -> p j k", k=16)
            VOP(v, lambda: v.tensor_scalar(gv2, gview, 1e30, 1.0, ALU.mult, ALU.add))
            VOP(v, lambda: v.tensor_scalar(gagb[:], gagb[:], -1.0, 1.0, ALU.max, ALU.min))
            VOP(v, lambda: v.tensor_scalar(
                gtmp.ap().rearrange("p (j k) -> p j k", k=16), gview, -1.0, None, ALU.mult))
            VOP(v, lambda: v.tensor_tensor(
                gtmp.ap().rearrange("p (j k) -> p j k", k=16), gview,
                gtmp.ap().rearrange("p (j k) -> p j k", k=16), ALU.max))
            VOP(v, lambda: v.tensor_scalar_max(gtmp[:], gtmp[:], 0.25))
            rec["gagb"] = VOP(v, lambda: v.tensor_mul(gagb[:], gagb[:], gtmp[:]))
            W(v, "g0")
            rec["gds"] = VOP(v, lambda: v.tensor_mul(gds[:], dsT[:], gagb[:, 16:32]))
            rec["asum"] = VOP(v, lambda: v.reduce_sum(
                asum[:], mnT.ap().rearrange("d (h n) -> d n h", n=NB), axis=AX.X))
            VOP(v, lambda: v.reduce_max(bwR[:], bwS[:], axis=AX.X))
            rec["bwsub"] = VOP(v, lambda: v.tensor_scalar(
                bwE[:], bwS[:], bwR[:], None, ALU.subtract))

        @block.scalar
        def _(s):
            W(s, "bwsub")
            rec["bwexp"] = COP(s, lambda: s.activation(bwE[:], bwE[:], AF.Exp))

        @block.vector
        def _(v):
            W(v, "bwexp")
            VOP(v, lambda: v.reduce_sum(bwR[:], bwE[:], axis=AX.X))
            VOP(v, lambda: v.reciprocal(bwR[:], bwR[:]))
            rec["bwsm"] = VOP(v, lambda: v.tensor_scalar(
                bwE[:], bwE[:], bwR[:], None, ALU.mult))

        @block.tensor
        def _(t):
            W(t, "gds")
            t.transpose(pA[0:96, 0:128], modacc[:], identb[:]).then_inc(semT, 1)
            rec["modtr"] = (semT, _n(semT, 1))
            W(t, "bwsm")
            t.transpose(pB[0:4, 0:16], bwE[0:16, 0:4], identb[0:16, 0:16]).then_inc(semT, 1)
            rec["wdtr"] = (semT, _n(semT, 1))
            W(t, "asum")
            t.matmul(pB[0:4, 16:17], asum[:], onesb[:], start=True, stop=True).then_inc(semT, 1)
            rec["actmm"] = (semT, _n(semT, 1))

        @block.vector
        def _(v):
            W(v, "actmm")
            rec["modRc"] = VOP(v, lambda: v.tensor_copy(out=modR[:], in_=pA[0:96, 0:128]))
            rec["wdTc"] = VOP(v, lambda: v.tensor_copy(out=wdT[:], in_=pB[0:4, 0:16]))
            VOP(v, lambda: v.tensor_scalar(
                activ[:], pB[0:4, 16:17], ATT_EPS, 1e30, ALU.subtract, ALU.mult))
            rec["activ"] = VOP(v, lambda: v.tensor_scalar(
                activ[:], activ[:], 1.0, 0.0, ALU.add, ALU.max))
            rec["activ"] = VOP(v, lambda: v.tensor_scalar_min(activ[:], activ[:], 1.0))

        @block.scalar
        def _(s):
            W(s, "modRc")
            DMA(semA, lambda: s.dma_start(out=modscr, in_=modR[:]))
            s.wait_ge(semA, _cur(semA))
            DMA(semA, lambda: s.dma_start(out=lns1[:], in_=modscr[0:16, :].rearrange(
                "r e -> (r e)").unsqueeze(0).to_broadcast([128, D])))
            rec["ln1tiles"] = DMA(semA, lambda: s.dma_start(
                out=lnc1[:], in_=modscr[16:32, :].rearrange(
                    "r e -> (r e)").unsqueeze(0).to_broadcast([128, D])))

        emit_ln("ln1", lnc1, lns1, None, None)

        if upto == "mod":
            emit_tap(["activ", "ln1tr%d" % (TI - 1)],
                     dict(modacc=modacc[:], gagb=gagb[:], wdT=wdT[:], activ=activ[:],
                          h1T=hT[:], X=Xs[:]))
            _st0.close()
            return nc

        _st0.close()

        if upto == "h1":
            emit_tap(["ln1tr%d" % (TI - 1)], dict(h1T=hT[:], X=Xs[:]))
            return nc

        _st1 = contextlib.ExitStack()
        ec1 = _st1.enter_context
        sqc = ec1(nc.sbuf_tensor("sqc", [128, NSLOT * T], BF16))
        wsraw = ec1(nc.sbuf_tensor("wsraw", [128, 2 * D], BF16))
        wstr = ec1(nc.sbuf_tensor("wstr", [128, 2 * D], BF16))
        mem4 = ec1(nc.sbuf_tensor("mem4", [128, 4 * 512], BF16))
        cwb = ec1(nc.sbuf_tensor("cwb", [128, NB * T], BF16))
        cwbf = ec1(nc.sbuf_tensor("cwbf", [4, T], BF16))
        nrme = ec1(nc.sbuf_tensor("nrme", [4, T], BF16))
        nrmr = ec1(nc.sbuf_tensor("nrmr", [4, T], BF16))
        mbuf = ec1(nc.sbuf_tensor("mbuf", [128, T], BF16))
        rbuf = ec1(nc.sbuf_tensor("rbuf", [128, T], BF16))
        ebuf = ec1(nc.sbuf_tensor("ebuf", [128, T], BF16))
        sqs4 = ec1(nc.sbuf_tensor("sqs4", [128, NB * T], BF16))
        outTe = ec1(nc.sbuf_tensor("outTe", [128, 2 * T], BF16))
        otok = ec1(nc.sbuf_tensor("otok", [128, 2 * T], BF16))

        def sqslot(h):
            return sqc[:, T * (16 + h % NSQ):T * (17 + h % NSQ)]

        def combslot(j):
            return sqc[:, T * j:T * (j + 1)]

        @block.gpsimd
        def _(g):
            W(g, "actmm")
            W(g, "activ")
            W(g, "bwexp")
            for h in range(2):
                rec["wqld%d" % h] = DMA(semWQ[h % 2], lambda h=h: g.dma_start(
                    out=wsraw[:, D * (h % 2):D * (h % 2 + 1)],
                    in_=wq_in[128 * h:128 * (h + 1), :]))

        for u in range(H + 4):
            @block.vector
            def _(v, u=u):
                if 4 <= u and u - 4 < H:
                    h = u - 4
                    W(v, "tY%d" % h)
                    rec["vcomb%d" % h] = VOP(v, lambda: v.tensor_copy(
                        out=combslot(h), in_=pD[:, :]))
                if 3 <= u and u - 3 < H:
                    h = u - 3
                    W(v, "acw%d" % h)
                    r = None
                    for n in range(NB):
                        r = VOP(v, lambda n=n: v.tensor_mul(
                            sqs4[:, T * n:T * (n + 1)], sqslot(h),
                            cwb[:, T * n:T * (n + 1)]))
                    rec["vsqs%d" % h] = r
                if 1 <= u and u - 1 < H:
                    h = u - 1
                    W(v, "tq%d" % h)
                    if h >= 1:
                        W(v, "cexp%d" % (h - 1))
                    qp = pA if h % 2 == 0 else pB
                    rec["vmin%d" % h] = VOP(v, lambda: v.tensor_scalar_min(
                        mbuf[:], qp[:, :], 0.0))

            @block.scalar
            def _(s, u=u):
                if 1 <= u and u - 1 < H:
                    h = u - 1
                    W(s, "tq%d" % h)
                    if h >= 1:
                        W(s, "vsq%d" % (h - 1))
                    qp = pA if h % 2 == 0 else pB
                    COP(s, lambda: s.activation(rbuf[:], qp[:, :], AF.Relu))
                    W(s, "vmin%d" % h)
                    rec["cexp%d" % h] = COP(s, lambda: s.activation(ebuf[:], mbuf[:], AF.Exp))

            @block.vector
            def _(v, u=u):
                if 1 <= u and u - 1 < H:
                    h = u - 1
                    W(v, "cexp%d" % h)
                    if h == 0:
                        W(v, "modk%d" % (C - 1))
                    rec["vsq%d" % h] = VOP(v, lambda: v.tensor_add(
                        sqslot(h), ebuf[:], rbuf[:]))
                if 2 <= u and u - 2 < H:
                    h = u - 2
                    W(v, "tn%d" % h)
                    rec["vnmax%d" % h] = VOP(v, lambda: v.tensor_scalar_max(
                        nrme[:], pC[0:4, :], ATT_EPS))
                    with nc.allow_low_precision(reason="bank-norm recip at bf16"):
                        VOP(v, lambda: v.reciprocal(nrmr[:], nrme[:]))
                    rec["vcw%d" % h] = VOP(v, lambda: v.tensor_scalar(
                        cwbf[:], nrmr[:], wdT[:, h:h + 1], activ[:], ALU.mult, ALU.mult))

            @block.scalar
            def _(s, u=u):
                if 2 <= u and u - 2 < H:
                    h = u - 2
                    W(s, "vcw%d" % h)
                    DMA(semA, lambda: s.dma_start(
                        out=cwscr[h:h + 1, :].rearrange("o (n t) -> (o n) t", n=NB),
                        in_=cwbf[:]))
                    s.wait_ge(semA, _cur(semA))
                    if h >= 1:
                        W(s, "vsqs%d" % (h - 1))
                    rec["acw%d" % h] = DMA(semA, lambda: s.dma_start(
                        out=cwb[:], in_=cwscr[h:h + 1, :].to_broadcast([128, NB * T])))

            @block.sync
            def _(y, u=u):
                if u < H:
                    h = u
                    W(y, "wqld%d" % h)
                    if h >= 2:
                        W(y, "tq%d" % (h - 2))
                    rec["wqtr%d" % h] = DMA(semTR[h % 2], lambda: y.dma_start_transpose(
                        wstr[:, D * (h % 2):D * (h % 2 + 1)].rearrange("p (k q) -> p k q", k=C),
                        wsraw[:, D * (h % 2):D * (h % 2 + 1)].rearrange("p (k q) -> p k q", k=C)))

            @block.tensor
            def _(t, u=u):
                if u < H:
                    h = u
                    W(t, "wqtr%d" % h)
                    if h == 0:
                        W(t, "ln1tr%d" % (TI - 1))
                        W(t, "activ")
                    if h >= 2:
                        W(t, "vmin%d" % (h - 2))
                        W(t, "cexp%d" % (h - 2))
                    qp = pA if h % 2 == 0 else pB
                    items = []
                    for k in range(C):
                        for hf in range(2):
                            items.append((
                                qp[:, 512 * hf:512 * (hf + 1)],
                                wstr[:, D * (h % 2) + 128 * k:D * (h % 2) + 128 * (k + 1)],
                                hTv[:, k, 512 * hf:512 * (hf + 1)],
                                k == 0, k == C - 1))
                    rec["tq%d" % h] = MMGROUP(t, items)
                if 1 <= u <= H:
                    h = u - 1
                    W(t, "vsq%d" % h)
                    if h >= 1:
                        W(t, "vnmax%d" % (h - 1))
                    items = [(pC[0:4, 512 * hf:512 * (hf + 1)],
                              mnT[:, 4 * h:4 * (h + 1)],
                              sqslot(h)[:, 512 * hf:512 * (hf + 1)],
                              True, True) for hf in range(2)]
                    rec["tn%d" % h] = MMGROUP(t, items)
                if 3 <= u and u - 3 < H:
                    h = u - 3
                    W(t, "vsqs%d" % h)
                    t.wait_ge(semME, 16 * (h + 1))
                    if h >= 1:
                        W(t, "vcomb%d" % (h - 1))
                    items = []
                    for n in range(NB):
                        for hf in range(2):
                            items.append((
                                pD[:, 512 * hf:512 * (hf + 1)],
                                mem4[:, 512 * (h % 4) + 128 * n:512 * (h % 4) + 128 * (n + 1)],
                                sqs4[:, T * n + 512 * hf:T * n + 512 * (hf + 1)],
                                n == 0, n == NB - 1))
                    rec["tY%d" % h] = MMGROUP(t, items)

            @block.gpsimd
            def _(g, u=u):
                if u + 2 < H:
                    h = u + 2
                    W(g, "wqtr%d" % (h - 2))
                    rec["wqld%d" % h] = DMA(semWQ[h % 2], lambda: g.dma_start(
                        out=wsraw[:, D * (h % 2):D * (h % 2 + 1)],
                        in_=wq_in[128 * h:128 * (h + 1), :]))
                if u < H:
                    h = u
                    if h >= 4:
                        W(g, "tY%d" % (h - 4))
                    rec["memld%d" % h] = SDMA(g, semME, lambda: g.dma_start(
                        out=mem4[:, 512 * (h % 4):512 * (h % 4 + 1)].rearrange(
                            "d (n e) -> d n e", n=NB),
                        in_=mem_in[:, h].transpose([1, 0, 2])))

        if upto == "attn":
            emit_tap(["vcomb%d" % (H - 1)], dict(comb=sqc[:, 0:16 * T], X=Xs[:]))
            _st1.close()
            return nc

        for u in range(H + 2):
            @block.vector
            def _(v, u=u):
                if 1 <= u <= H:
                    p = u - 1
                    W(v, "two%d" % p)
                    wp = pA if p % 2 == 0 else pB
                    rec["vwoe%d" % p] = VOP(v, lambda: v.tensor_scalar(
                        outTe[:, T * (p % 2):T * (p % 2 + 1)], wp[:, :],
                        gagb[:, p:p + 1], None, ALU.mult))
                if u >= 2:
                    p = u - 2
                    W(v, "wotr2%d" % p)
                    rec["vxadd%d" % p] = VOP(v, lambda: v.tensor_add(
                        Xv[:, :, 128 * p:128 * (p + 1)],
                        Xv[:, :, 128 * p:128 * (p + 1)],
                        otok[:, T * (p % 2):T * (p % 2 + 1)].rearrange(
                            "p (ti q) -> p ti q", ti=TI)))

            @block.gpsimd
            def _(g, u=u):
                if u < H:
                    p = u
                    if p < 2:
                        W(g, "wqtr%d" % (H - 2 + p))
                    else:
                        W(g, "wotr%d" % (p - 2))
                    rec["wold%d" % p] = DMA(semWQ[p % 2], lambda: g.dma_start(
                        out=wsraw[:, D * (p % 2):D * (p % 2 + 1)],
                        in_=wo_in[128 * p:128 * (p + 1), :]))

            @block.sync
            def _(y, u=u):
                if 1 <= u <= H:
                    p = u - 1
                    W(y, "vwoe%d" % p)
                    if p >= 2:
                        W(y, "vxadd%d" % (p - 2))
                    rec["wotr2%d" % p] = DMA(semOT[p % 2], lambda: y.dma_start_transpose(
                        otok[:, T * (p % 2):T * (p % 2 + 1)].rearrange("p (ti q) -> p ti q", ti=TI),
                        outTe[:, T * (p % 2):T * (p % 2 + 1)].rearrange("p (ti q) -> p ti q", ti=TI)))
                if u < H:
                    p = u
                    W(y, "wold%d" % p)
                    if p >= 2:
                        W(y, "two%d" % (p - 2))
                    rec["wotr%d" % p] = DMA(semTR[p % 2], lambda: y.dma_start_transpose(
                        wstr[:, D * (p % 2):D * (p % 2 + 1)].rearrange("p (k q) -> p k q", k=C),
                        wsraw[:, D * (p % 2):D * (p % 2 + 1)].rearrange("p (k q) -> p k q", k=C)))

            @block.tensor
            def _(t, u=u):
                if u < H:
                    p = u
                    W(t, "wotr%d" % p)
                    if p == 0:
                        W(t, "vcomb%d" % (H - 1))
                        W(t, "vmin%d" % (H - 1))
                        W(t, "cexp%d" % (H - 1))
                    if p >= 2:
                        W(t, "vwoe%d" % (p - 2))
                    wp = pA if p % 2 == 0 else pB
                    items = []
                    for j in range(C):
                        for hf in range(2):
                            items.append((
                                wp[:, 512 * hf:512 * (hf + 1)],
                                wstr[:, D * (p % 2) + 128 * j:D * (p % 2) + 128 * (j + 1)],
                                combslot(j)[:, 512 * hf:512 * (hf + 1)],
                                j == 0, j == C - 1))
                    rec["two%d" % p] = MMGROUP(t, items)

        _st1.close()

        if upto == "x1":
            emit_tap(["vxadd%d" % (H - 1)], dict(X=Xs[:]))
            return nc

        _st2 = contextlib.ExitStack()
        ec2 = _st2.enter_context
        lnc2 = ec2(nc.sbuf_tensor("lnc2", [128, D], BF16))
        lns2 = ec2(nc.sbuf_tensor("lns2", [128, D], BF16))
        fuse = ec2(nc.sbuf_tensor("fuse", [128, OC2 * T], BF16))
        g8 = ec2(nc.sbuf_tensor("g8", [128, 2 * D], U8))
        u8 = ec2(nc.sbuf_tensor("u8", [128, 2 * D], U8))
        d8 = ec2(nc.sbuf_tensor("d8", [128, 2 * OC2 * 128], U8))
        sgb = ec2(nc.sbuf_tensor("sg", [128, 2 * T], BF16))
        outTe2 = ec2(nc.sbuf_tensor("outTe2", [128, 2 * T], BF16))
        otok2 = ec2(nc.sbuf_tensor("otok2", [128, 2 * T], BF16))

        @block.scalar
        def _(s):
            W(s, "vxadd%d" % (H - 1))
            DMA(semA, lambda: s.dma_start(out=lns2[:], in_=modscr[48:64, :].rearrange(
                "r e -> (r e)").unsqueeze(0).to_broadcast([128, D])))
            rec["ln2tiles"] = DMA(semA, lambda: s.dma_start(
                out=lnc2[:], in_=modscr[64:80, :].rearrange(
                    "r e -> (r e)").unsqueeze(0).to_broadcast([128, D])))

        emit_ln("ln2", lnc2, lns2, "ln1", "vxadd%d" % (H - 1))

        if upto == "h2":
            emit_tap(["ln2tr%d" % (TI - 1)], dict(h2T=hT[:], X=Xs[:]))
            _st2.close()
            return nc

        for oh in range(2):
            for u in range(OC2 + 1):
                @block.scalar
                def _(s, u=u, oh=oh):
                    if 1 <= u:
                        oc = u - 1
                        ocg = OC2 * oh + oc
                        W(s, "tg_%d_%d" % (oh, oc))
                        rec["csil_%d_%d" % (oh, oc)] = COP(s, lambda: s.activation(
                            sgb[:, T * (oc % 2):T * (oc % 2 + 1)], pA[:, :], AF.Silu,
                            scale=gsT[:, ocg:ocg + 1]))

                @block.vector
                def _(v, u=u, oh=oh):
                    if 1 <= u:
                        oc = u - 1
                        ocg = OC2 * oh + oc
                        W(v, "tu_%d_%d" % (oh, oc))
                        W(v, "csil_%d_%d" % (oh, oc))
                        if oh == 1 and oc == 0:
                            W(v, "td_0_%d" % (H - 1))
                        rec["vfu_%d_%d" % (oh, oc)] = VOP(v, lambda: v.scalar_tensor_tensor(
                            fuse[:, T * oc:T * (oc + 1)], pB[:, :], usT[:, ocg:ocg + 1],
                            sgb[:, T * (oc % 2):T * (oc % 2 + 1)], ALU.mult, ALU.mult))

                @block.gpsimd
                def _(g, u=u, oh=oh):
                    if u < OC2:
                        oc = u
                        ocg = OC2 * oh + oc
                        if ocg == 0:
                            W(g, "two%d" % (H - 1))
                            W(g, "vxadd%d" % (H - 1))
                        if ocg >= 2:
                            po, pc_ = (ocg - 2) // OC2, (ocg - 2) % OC2
                            W(g, "tg_%d_%d" % (po, pc_))
                        rec["g8_%d_%d" % (oh, oc)] = SDMA(g, semG8, lambda: g.dma_start(
                            out=g8[:, D * (oc % 2):D * (oc % 2 + 1)].bitcast(F8).rearrange(
                                "q (k e) -> q k e", k=C),
                            in_=gw_in[:, 128 * ocg:128 * (ocg + 1)].rearrange(
                                "(k q) e -> q k e", q=128)))
                        if ocg >= 2:
                            po, pc_ = (ocg - 2) // OC2, (ocg - 2) % OC2
                            W(g, "tu_%d_%d" % (po, pc_))
                        rec["u8_%d_%d" % (oh, oc)] = SDMA(g, semU8, lambda: g.dma_start(
                            out=u8[:, D * (oc % 2):D * (oc % 2 + 1)].bitcast(F8).rearrange(
                                "q (k e) -> q k e", k=C),
                            in_=uw_in[:, 128 * ocg:128 * (ocg + 1)].rearrange(
                                "(k q) e -> q k e", q=128)))

                @block.tensor
                def _(t, u=u, oh=oh):
                    if u < OC2:
                        oc = u
                        W(t, "g8_%d_%d" % (oh, oc))
                        if oh + oc == 0:
                            W(t, "ln2tr%d" % (TI - 1))
                            W(t, "vwoe%d" % (H - 1))
                        if oc >= 1 or oh == 1:
                            po, pc_ = (oh, oc - 1) if oc >= 1 else (0, OC2 - 1)
                            W(t, "csil_%d_%d" % (po, pc_))
                        items = []
                        for k in range(C):
                            for hf in range(2):
                                items.append((
                                    pA[:, 512 * hf:512 * (hf + 1)],
                                    g8[:, D * (oc % 2) + 128 * k:D * (oc % 2) + 128 * (k + 1)].bitcast(F8),
                                    hTv[:, k, 512 * hf:512 * (hf + 1)],
                                    k == 0, k == C - 1))
                        rec["tg_%d_%d" % (oh, oc)] = MMGROUP(t, items)
                        W(t, "u8_%d_%d" % (oh, oc))
                        if oc >= 1 or oh == 1:
                            po, pc_ = (oh, oc - 1) if oc >= 1 else (0, OC2 - 1)
                            W(t, "vfu_%d_%d" % (po, pc_))
                        items = []
                        for k in range(C):
                            for hf in range(2):
                                items.append((
                                    pB[:, 512 * hf:512 * (hf + 1)],
                                    u8[:, D * (oc % 2) + 128 * k:D * (oc % 2) + 128 * (k + 1)].bitcast(F8),
                                    hTv[:, k, 512 * hf:512 * (hf + 1)],
                                    k == 0, k == C - 1))
                        rec["tu_%d_%d" % (oh, oc)] = MMGROUP(t, items)

            for u in range(H + 2):
                @block.vector
                def _(v, u=u, oh=oh):
                    if 1 <= u <= H:
                        p = u - 1
                        W(v, "td_%d_%d" % (oh, p))
                        dp = pC if p % 2 == 0 else pD
                        rec["vde_%d_%d" % (oh, p)] = VOP(v, lambda: v.tensor_scalar(
                            outTe2[:, T * (p % 2):T * (p % 2 + 1)], dp[:, :],
                            gds[:, p:p + 1], None, ALU.mult))
                    if u >= 2:
                        p = u - 2
                        W(v, "dtr_%d_%d" % (oh, p))
                        rec["vda_%d_%d" % (oh, p)] = VOP(v, lambda: v.tensor_add(
                            Xv[:, :, 128 * p:128 * (p + 1)],
                            Xv[:, :, 128 * p:128 * (p + 1)],
                            otok2[:, T * (p % 2):T * (p % 2 + 1)].rearrange(
                                "p (ti q) -> p ti q", ti=TI)))

                @block.gpsimd
                def _(g, u=u, oh=oh):
                    if u < H:
                        p = u
                        if oh == 1 or p >= 2:
                            po, pp = (oh, p - 2) if p >= 2 else (0, H - 2 + p)
                            W(g, "td_%d_%d" % (po, pp))
                        rec["d8_%d_%d" % (oh, p)] = SDMA(g, semD8, lambda: g.dma_start(
                            out=d8[:, OC2 * 128 * (p % 2):OC2 * 128 * (p % 2 + 1)].bitcast(F8).rearrange(
                                "q (oc e) -> q oc e", oc=OC2),
                            in_=dw_in[3072 * oh:3072 * (oh + 1),
                                      128 * p:128 * (p + 1)].rearrange(
                                "(oc q) e -> q oc e", q=128)))

                @block.sync
                def _(y, u=u, oh=oh):
                    if 1 <= u <= H:
                        p = u - 1
                        W(y, "vde_%d_%d" % (oh, p))
                        if p >= 2:
                            W(y, "vda_%d_%d" % (oh, p - 2))
                        elif oh == 1:
                            W(y, "vda_0_%d" % (H - 2 + p))
                        rec["dtr_%d_%d" % (oh, p)] = DMA(semOT[p % 2], lambda: y.dma_start_transpose(
                            otok2[:, T * (p % 2):T * (p % 2 + 1)].rearrange("p (ti q) -> p ti q", ti=TI),
                            outTe2[:, T * (p % 2):T * (p % 2 + 1)].rearrange("p (ti q) -> p ti q", ti=TI)))
                    if u >= 2 and oh == 1:
                        p = u - 2
                        W(y, "vda_1_%d" % p)
                        rec["st%d" % p] = DMA(semST, lambda: y.dma_start(
                            out=out_d.rearrange("(ti q) d -> q ti d", q=128)[:, :, 128 * p:128 * (p + 1)],
                            in_=Xv[:, :, 128 * p:128 * (p + 1)]))

                @block.tensor
                def _(t, u=u, oh=oh):
                    if u < H:
                        p = u
                        W(t, "d8_%d_%d" % (oh, p))
                        if p == 0:
                            W(t, "vfu_%d_%d" % (oh, OC2 - 1))
                            if oh == 0:
                                W(t, "vnmax%d" % (H - 1))
                                W(t, "vcomb%d" % (H - 1))
                        if p >= 2:
                            W(t, "vde_%d_%d" % (oh, p - 2))
                        elif oh == 1:
                            W(t, "vde_0_%d" % (H - 2 + p))
                        dp = pC if p % 2 == 0 else pD
                        items = []
                        for oc in range(OC2):
                            for hf in range(2):
                                items.append((
                                    dp[:, 512 * hf:512 * (hf + 1)],
                                    d8[:, OC2 * 128 * (p % 2) + 128 * oc:
                                       OC2 * 128 * (p % 2) + 128 * (oc + 1)].bitcast(F8),
                                    fuse[:, T * oc + 512 * hf:T * oc + 512 * (hf + 1)],
                                    oc == 0, oc == OC2 - 1))
                        rec["td_%d_%d" % (oh, p)] = MMGROUP(t, items)

        _st2.close()

        @block.sync
        def _(y):
            y.wait_ge(semST, 16 * H)

    return nc


def _consts():
    cst = np.zeros((128, 129), np.float32)
    cst[:, 0:128] = np.eye(128, dtype=np.float32)
    cst[:, 128] = 1.0
    return cst


def kernel(L0_attn_hidden_states, L0_attn_Wq, L0_attn_Wk, L0_attn_Wv, L0_attn_Wo,
           L0_attn_bank_weights, L0_attn_memories, L0_attn_memory_norms,
           L0_ffn_gate_w, L0_ffn_up_w, L0_ffn_down_w, L0_ffn_gate_scale,
           L0_ffn_up_scale, L0_ffn_down_scale, L0_c, L0_wmod):
    nc = build_nc()
    asnp = lambda a: np.ascontiguousarray(np.asarray(a, dtype=np.float32))
    x = asnp(L0_attn_hidden_states)
    shared = {
        "wq": asnp(L0_attn_Wq), "wo": asnp(L0_attn_Wo),
        "bw": asnp(L0_attn_bank_weights), "mem": asnp(L0_attn_memories),
        "mn": asnp(L0_attn_memory_norms), "gw": asnp(L0_ffn_gate_w),
        "uw": asnp(L0_ffn_up_w), "dw": asnp(L0_ffn_down_w),
        "gs": asnp(L0_ffn_gate_scale).reshape(1, FI),
        "us": asnp(L0_ffn_up_scale).reshape(1, FI),
        "ds": asnp(L0_ffn_down_scale).reshape(1, D),
        "wm": asnp(L0_wmod), "cst": _consts(),
    }
    cfull = asnp(L0_c)
    in_maps = []
    for c in range(8):
        b, th = c // 2, c % 2
        m = dict(shared)
        m["x"] = np.ascontiguousarray(x[b, T * th:T * (th + 1), :])
        m["cb"] = np.ascontiguousarray(cfull[b:b + 1, :])
        in_maps.append(m)
    res = run_bass_kernel_spmd(nc, in_maps, core_ids=list(range(8)))
    out = np.empty((4, 2048, D), np.float32)
    for c in range(8):
        b, th = c // 2, c % 2
        out[b, T * th:T * (th + 1), :] = res.results[c]["out"]
    return out
